# version 04
# baseline (speedup 1.0000x reference)
import numpy as np
from contextlib import ExitStack

import concourse.bass as bass
import concourse.mybir as mybir
from concourse.bass_utils import run_bass_kernel_spmd

F32 = mybir.dt.float32
BF16 = mybir.dt.bfloat16
AF = mybir.ActivationFunctionType
ALU = mybir.AluOpType
AX = mybir.AxisListType

B, S, D = 2, 8192, 1024
HC = 8
TOK = 2048
NGR = 16
OWN_GR = 4
NE = 34
NI = 512
NTB = 4
NU = NE * NTB
NST = 8
NKB = 64
MAXD = 10
BSC = 0.01

ENGINE_OWN_SEM = {"T": "tmm", "V": "vev", "S": "sev", "G": "gco"}
DMA_SEMS = ("dxa", "dxb", "dff", "dffr", "dta", "dtb", "dra", "drb",
            "dwa", "dwb", "dca", "dcb", "dba", "dbb", "dgr", "dxr",
            "dor", "dmisc", "dgw")
CORE_SEMS = ("tmm", "vev", "sev", "gco")
ALL_SEMS = CORE_SEMS + DMA_SEMS


class Walk:

    def __init__(self, key, eng, sems):
        self.key = key
        self.eng = eng
        self.sems = sems
        self.c = {s: 0 for s in ALL_SEMS}
        self.ev = {}
        self.waited = {s: 0 for s in ALL_SEMS}

    def snap(self):
        return dict(self.c)

    def bar(self, snap):
        return [(s, v) for s, v in snap.items() if v > 0]

    def op(self, key, fn, waits=(), inc=None, inc_amt=None, tag=None, drain=False):
        post = None
        if inc is not None:
            amt = inc_amt if inc_amt is not None else (16 if inc in DMA_SEMS else 1)
            self.c[inc] += amt
            post = self.c[inc]
            if tag is not None:
                self.ev[tag] = (inc, post)
        if self.key == key:
            own = ENGINE_OWN_SEM.get(self.key)
            for wv in waits:
                if wv is None:
                    continue
                if isinstance(wv, tuple) and isinstance(wv[0], str) and wv[0] in ALL_SEMS:
                    s, v = wv
                else:
                    s, v = self.ev[wv]
                if s == own:
                    continue
                if v <= self.waited[s]:
                    continue
                self.waited[s] = v
                self.eng.wait_ge(self.sems[s], v)
            if fn is not None:
                if drain:
                    self.eng.drain()
                ins = fn(self.eng)
                if inc is not None:
                    ins.then_inc(self.sems[inc], amt)
        return post


def _band_instances():
    out = []
    for st in range(NST):
        insts = []
        for rel in (128, 0, -128, -256):
            kb = (st * 256 - rel) // 128
            if kb < 0:
                kb = NKB - 1
            insts.append(kb)
        out.append(insts)
    return out


BAND_KBS = _band_instances()


def build_program(phases=3, debug_scratch=False):
    nc = bass.Bass("TRN2", target_bir_lowering=False)

    x_d = nc.dram_tensor("x", [S, D], F32, kind="ExternalInput")
    wt_d = nc.dram_tensor("wt", [D, D], F32, kind="ExternalInput")
    battn_d = nc.dram_tensor("battn", [D, 1], F32, kind="ExternalInput")
    gw_d = nc.dram_tensor("gw", [D, 32], F32, kind="ExternalInput")
    gb_d = nc.dram_tensor("gb", [1, 32], F32, kind="ExternalInput")
    moe_d = nc.dram_tensor("moe", [NE, 3, NI * D], F32, kind="ExternalInput")
    bands_d = nc.dram_tensor("bands", [NST, 128, 4, 256], F32, kind="ExternalInput")
    out_d = nc.dram_tensor("out", [TOK, D], F32, kind="ExternalOutput")

    skind = "ExternalOutput" if debug_scratch else "Internal"
    ffm_d = nc.dram_tensor("ffm", [128, HC, S], BF16, kind=skind)
    ftm_d = nc.dram_tensor("ftm", [128, NKB, D], BF16, kind=skind)
    gscr_d = nc.dram_tensor("gscr", [128, 16, D], BF16, kind=skind)
    comb_dbg = None
    if debug_scratch:
        comb_dbg = nc.dram_tensor("combdbg", [32, TOK], BF16, kind="ExternalOutput")
        negc_dbg = nc.dram_tensor("negcdbg", [2, TOK], BF16, kind="ExternalOutput")
        rs_dbg = nc.dram_tensor("rsdbg", [NST, 128, 4], F32, kind="ExternalOutput")

    with ExitStack() as top:
        E = top.enter_context
        sems = {s: E(nc.semaphore(s)) for s in ALL_SEMS}

        ones_row = E(nc.sbuf_tensor("ones_row", [1, 128], BF16))
        ones_col = E(nc.sbuf_tensor("ones_col", [128, 1], BF16))
        ident_bf = E(nc.sbuf_tensor("ident_bf", [128, 128], BF16))
        ident_f32 = E(nc.sbuf_tensor("ident_f32", [128, 128], F32))
        negCf = E(nc.sbuf_tensor("negCf", [1, TOK], BF16))
        negCn = E(nc.sbuf_tensor("negCn", [1, TOK], BF16))
        battn_sb = E(nc.sbuf_tensor("battn_sb", [128, HC], F32))
        gw_sb = E(nc.sbuf_tensor("gw_sb", [128, HC, 32], BF16))

        def walks(block, fn, init=None, init_ev=None):
            def run(key):
                def _f(eng):
                    w = Walk(key, eng, sems)
                    if init:
                        w.c.update(init)
                    if init_ev:
                        w.ev.update(init_ev)
                    fn(w)
                return _f
            block.tensor(run("T"))
            block.vector(run("V"))
            block.scalar(run("S"))
            block.sync(run("Y"))
            block.gpsimd(run("G"))

        bar1 = {}
        bar2 = {}
        evc1 = {}

        def emit_moe_loads(w, e, first_waits):
            O = w.op
            par = e % 2
            wsem = "dwa" if par == 0 else "dwb"
            srcs = [
                lambda e=e: moe_d[e, 0, :].rearrange("(hc p i) -> p hc i", p=128, i=NI),
                lambda e=e: moe_d[e, 1, :].rearrange("(hc p i) -> p hc i", p=128, i=NI),
                lambda e=e: moe_d[e, 2, :].rearrange("(ic p h) -> p ic h", p=128, h=D),
            ]
            for m, dst in ((0, wg_sb2), (1, wu_sb2), (2, wd_sb2)):
                O("G", lambda g, dst=dst, par=par, src=srcs[m]: g.dma_start(
                    dst[par][:, :, :], src()),
                  waits=first_waits if m == 0 else (), inc=wsem,
                  tag=(("ldw", e) if m == 2 else None))

        def emit_crow_load(w, e, waits):
            O = w.op
            par = e % 2
            csem = "dca" if par == 0 else "dcb"
            O("Y", lambda y, e=e, par=par: y.dma_start(crow[par][:, :], combT[e:e + 1, :]),
              waits=waits, inc=csem, tag=("ldc", e))

        def ph1(w):
            O = w.op
            emit_moe_loads(w, 0, [])
            emit_moe_loads(w, 1, [])
            O("G", lambda g: g.memset(ones_row[:, :], 1.0), inc="gco", tag="c_or")
            O("G", lambda g: g.memset(ones_col[:, :], 1.0), inc="gco", tag="c_oc")
            O("G", lambda g: g.iota(x_stage[0][:, 0, 0:128], [[1, 128]], channel_multiplier=0,
                                    allow_small_or_imprecise_dtypes=True), inc="gco", tag="c_i1")
            O("G", lambda g: g.iota(x_stage[0][:, 1, 0:128], [[0, 128]], channel_multiplier=1,
                                    allow_small_or_imprecise_dtypes=True), inc="gco", tag="c_i2")
            O("V", lambda v: v.tensor_tensor(ident_bf[:, :], x_stage[0][:, 0, 0:128],
                                             x_stage[0][:, 1, 0:128], ALU.is_equal),
              waits=["c_i1", "c_i2"], inc="vev", tag="c_idb")
            O("V", lambda v: v.tensor_tensor(ident_f32[:, :], x_stage[0][:, 0, 0:128],
                                             x_stage[0][:, 1, 0:128], ALU.is_equal),
              inc="vev", tag="c_idf")
            O("G", lambda g: g.dma_start(wt_sb[:, :, :],
                                         wt_d[:, :].rearrange("(hc p) ho -> p hc ho", p=128)),
              inc="dmisc", tag="ld_wt")
            O("G", lambda g: g.dma_start(gw_sb[:, :, :],
                                         gw_d[:, :].rearrange("(hc p) e -> p hc e", p=128)),
              inc="dmisc", tag="ld_gw")
            O("G", lambda g: g.dma_start(gb_bf[:, :], gb_d[:, :]), inc="dmisc", tag="ld_gb")
            O("Y", lambda y: y.dma_start(battn_sb[:, :],
                                         battn_d[:, :].rearrange("(hc p) o -> p (hc o)", p=128),
                                         allow_slow_non_contiguous=True),
              inc="dmisc", tag="ld_ba")
            w.ev["ld_all"] = ("dmisc", w.c["dmisc"])
            O("T", lambda t: t.matmul(ps_gate[:, :32], ones_row[:, :], gb_bf[:, :],
                                      start=True, stop=True),
              waits=["ld_all", "c_or"], inc="tmm", tag="mm_br")
            O("V", lambda v: v.tensor_copy(biasrep[:, :], ps_gate[:, :32]),
              waits=["mm_br"], inc="vev", tag="ev_br")

            trcnt = [0]
            trfree = [None, None, None, None]

            def transpose_to(src_fn, dst_fn, dst_eng, extra_waits, tagbase, prows=128):
                b = trcnt[0] % 4
                O("T", lambda t, b=b, prows=prows: t.transpose(
                    ps_tr[b][0:prows, :128], src_fn(), ident_bf[:, :]),
                  waits=list(extra_waits) + ["c_idb", trfree[b]],
                  inc="tmm", tag=("tr",) + tagbase)
                if dst_eng == "S":
                    O("S", lambda s, b=b, prows=prows: s.activation(
                        dst_fn(), ps_tr[b][0:prows, :128], AF.Copy),
                      waits=[("tr",) + tagbase], inc="sev", tag=("trev",) + tagbase)
                else:
                    O("V", lambda v, b=b, prows=prows: v.tensor_copy(
                        dst_fn(), ps_tr[b][0:prows, :128]),
                      waits=[("tr",) + tagbase], inc="vev", tag=("trev",) + tagbase)
                trfree[b] = ("trev",) + tagbase
                trcnt[0] += 1

            prev_log = [None]

            def emit_routing():
                O = w.op
                O("S", lambda s: s.activation(scores_sb[:, :, :], logits_sb[:, :, :], AF.Sigmoid),
                  waits=[("evlog", 3, 3)], inc="sev", tag="s_scores")
                PAIRS = [(0, 1), (0, 2), (0, 3), (1, 2), (1, 3), (2, 3)]
                for t16 in range(16):
                    O("V", lambda v, t16=t16: v.tensor_add(rt_sc[:, :], scores_sb[:, t16, :],
                                                           biasrep[:, :]),
                      waits=["s_scores", "ev_br"], inc="vev")
                    for i, (a, b) in enumerate(PAIRS):
                        O("V", lambda v, i=i, a=a, b=b: v.tensor_add(
                            rt_pair[:, i * 8:(i + 1) * 8], rt_sc[:, a:32:4], rt_sc[:, b:32:4]),
                          inc="vev", drain=(i == 0))
                    O("V", lambda v: v.tensor_max(rt_m1[:, :], rt_pair[:, 0:8], rt_pair[:, 8:16]),
                      inc="vev")
                    O("V", lambda v: v.tensor_max(rt_m2[:, :], rt_pair[:, 16:24], rt_pair[:, 24:32]),
                      inc="vev")
                    O("V", lambda v: v.tensor_max(rt_m3[:, :], rt_pair[:, 32:40], rt_pair[:, 40:48]),
                      inc="vev")
                    O("V", lambda v: v.tensor_max(rt_m1[:, :], rt_m1[:, :], rt_m2[:, :]),
                      inc="vev", drain=True)
                    O("V", lambda v: v.tensor_max(rt_gs[:, :], rt_m1[:, :], rt_m3[:, :]),
                      inc="vev", drain=True)
                    O("V", lambda v: v.max(out=rt_gs8[:, :], in_=rt_gs[:, :]),
                      inc="vev", drain=True)
                    O("V", lambda v: v.tensor_scalar(out=rt_gmask[:, :], in0=rt_gs[:, :],
                                                     scalar1=rt_gs8[:, 3:4], scalar2=None,
                                                     op0=ALU.is_ge), inc="vev", drain=True)
                    O("V", lambda v: v.tensor_copy(
                        rt_smask[:, :].rearrange("p (g j) -> p g j", j=4),
                        rt_gmask[:, :, None].broadcast_to([128, 8, 4])), inc="vev", drain=True)
                    O("V", lambda v: v.tensor_mul(rt_masked[:, :], rt_sc[:, :], rt_smask[:, :]),
                      inc="vev", drain=True)
                    O("V", lambda v: v.max(out=rt_tk8[:, :], in_=rt_masked[:, :]),
                      inc="vev", drain=True)
                    O("V", lambda v: v.tensor_scalar(out=rt_emask[:, :], in0=rt_masked[:, :],
                                                     scalar1=rt_tk8[:, 7:8], scalar2=None,
                                                     op0=ALU.is_ge), inc="vev", drain=True)
                    O("V", lambda v, t16=t16: v.tensor_mul(rt_w8[:, :], scores_sb[:, t16, :],
                                                           rt_emask[:, :]), inc="vev", drain=True)
                    O("V", lambda v: v.reduce_sum(rt_wsum[:, :], rt_w8[:, :], axis=AX.X),
                      inc="vev", drain=True)
                    O("V", lambda v: v.tensor_scalar(out=rt_wsum[:, :], in0=rt_wsum[:, :],
                                                     scalar1=1e-20, scalar2=None, op0=ALU.add),
                      inc="vev", drain=True)
                    O("V", lambda v: v.reciprocal(rt_rec[:, :], rt_wsum[:, :]),
                      inc="vev", drain=True)
                    O("V", lambda v, t16=t16: v.tensor_scalar(
                        out=rt_comb[:, t16, :], in0=rt_w8[:, :],
                        scalar1=rt_rec[:, :], scalar2=2.5,
                        op0=ALU.mult, op1=ALU.mult),
                      inc="vev", tag=("comb", t16), drain=True)
            def emit_ldx(gr):
                xb = gr % 2
                xsem = "dxa" if xb == 0 else "dxb"
                xw = [("xn", gr - 2, 3)] if gr >= 2 else (["c_idf"] if gr == 0 else [])
                O("Y", lambda y, gr=gr, xb=xb: y.dma_start(
                    x_stage[xb][:, :, :],
                    x_d[gr * 512:(gr + 1) * 512, :].rearrange("(a p) d -> p a d", p=128)),
                  waits=xw, inc=xsem, tag=("ldx", gr))

            def emit_stats(gr):
                xb = gr % 2
                xsem = "dxa" if xb == 0 else "dxb"
                sp = gr % 2
                ldxw = [(xsem, w.ev[("ldx", gr)][1])]
                O("V", lambda v, xb=xb: v.reduce_sum(
                    stat_sum[:, :, None], x_stage[xb][:, :, :], axis=AX.X),
                  waits=ldxw, inc="vev")
                sqw = [("tr", gr - 2, 3, 7, "xn")] if gr >= 2 else []
                for tt in range(4):
                    O("V", lambda v, xb=xb, tt=tt: v.tensor_mul(
                        xn_stage[xb][:, 3, :], x_stage[xb][:, tt, :], x_stage[xb][:, tt, :]),
                      waits=sqw if tt == 0 else [], inc="vev")
                    O("V", lambda v, tt=tt, xb=xb: v.reduce_sum(
                        stat_sumsq[:, tt:tt + 1], xn_stage[xb][:, 3, :], axis=AX.X),
                      inc="vev", drain=True)
                O("V", lambda v: v.tensor_scalar(out=stat_mean[:, :], in0=stat_sum[:, :],
                                                 scalar1=1.0 / D, scalar2=None, op0=ALU.mult),
                  inc="vev", drain=True)
                O("V", lambda v: v.tensor_scalar(out=stat_msq[:, :], in0=stat_sumsq[:, :],
                                                 scalar1=1.0 / D, scalar2=1e-5,
                                                 op0=ALU.mult, op1=ALU.add), inc="vev")
                O("V", lambda v: v.scalar_tensor_tensor(
                    out=stat_var[:, :], in0=stat_mean[:, :], scalar=-1.0,
                    in1=stat_mean[:, :], op0=ALU.mult, op1=ALU.mult), inc="vev", drain=True)
                O("V", lambda v: v.tensor_add(stat_var[:, :], stat_msq[:, :],
                                              stat_var[:, :]), inc="vev", drain=True)
                O("V", lambda v: v.reciprocal(stat_ln[:, :], stat_var[:, :]),
                  inc="vev", tag=("var", gr), drain=True)
                O("S", lambda s, sp=sp: s.activation(stat_rstd[sp][:, :], stat_ln[:, :],
                                                     AF.Sqrt),
                  waits=[("var", gr)], inc="sev", tag=("rstd", gr))
                O("V", lambda v, sp=sp: v.scalar_tensor_tensor(
                    out=stat_nmr[sp][:, :], in0=stat_mean[:, :], scalar=-1.0,
                    in1=stat_rstd[sp][:, :], op0=ALU.mult, op1=ALU.mult),
                  waits=[("rstd", gr)], inc="vev", tag=("nmr", gr), drain=True)

            emit_ldx(0)
            emit_stats(0)
            for gr in range(NGR):
                xb = gr % 2
                sp = gr % 2
                if gr + 1 < NGR:
                    emit_ldx(gr + 1)
                    emit_stats(gr + 1)
                for tt in range(4):
                    xnw = [("nmr", gr)]
                    if gr >= 2:
                        xnw.append(("tr", gr - 2, tt, 7, "xn"))
                    O("V", lambda v, xb=xb, tt=tt, sp=sp: v.tensor_scalar(
                        out=xn_stage[xb][:, tt, :], in0=x_stage[xb][:, tt, :],
                        scalar1=stat_rstd[sp][:, tt:tt + 1], scalar2=stat_nmr[sp][:, tt:tt + 1],
                        op0=ALU.mult, op1=ALU.add),
                      waits=xnw, inc="vev", tag=("xn", gr, tt), drain=(tt == 0))

                    for hc in range(HC):
                        if gr < OWN_GR:
                            dst = (lambda gr=gr, tt=tt, hc=hc:
                                   xn_own[:, hc, gr * 512 + tt * 128: gr * 512 + (tt + 1) * 128])
                        else:
                            dst = (lambda xb=xb, tt=tt, hc=hc:
                                   xnf_stage[xb][:, hc, tt * 128:(tt + 1) * 128])
                        xtra = [("xn", gr, tt)]
                        if gr >= 2 and gr - 2 >= OWN_GR:
                            xtra.append(("mmf", gr - 2, 7))
                        transpose_to(
                            (lambda xb=xb, tt=tt, hc=hc:
                             xn_stage[xb][:, tt, hc * 128:(hc + 1) * 128]),
                            dst, "S", xtra, (gr, tt, hc, "xn"))

                for hoc in range(HC):
                    fb = hoc % 2
                    for hc in range(HC):
                        if gr < OWN_GR:
                            rhs = (lambda gr=gr, hc=hc: xn_own[:, hc, gr * 512:(gr + 1) * 512])
                        else:
                            rhs = (lambda xb=xb, hc=hc: xnf_stage[xb][:, hc, :])
                        ws = [("trev", gr, 3, hc, "xn")]
                        if hoc >= 2 and hc == 0:
                            ws.append(("evf", gr, hoc - 2))
                        if hc == 0:
                            ws.append("ld_all")
                        O("T", lambda t, hoc=hoc, fb=fb, hc=hc, rhs=rhs: t.matmul(
                            ps_f[fb][:, :], wt_sb[:, hc, hoc * 128:(hoc + 1) * 128], rhs(),
                            start=(hc == 0), stop=(hc == 7)),
                          waits=ws, inc=("tmm" if hc == 7 else None),
                          tag=(("mmf", gr, hoc) if hc == 7 else None))
                    ws = [("mmf", gr, hoc), "ld_all"]
                    if gr >= 2 and hoc == 0:
                        ws += [("fsfree", gr - 2), ("stf", gr - 2)]
                        if gr - 2 < OWN_GR:
                            ws.append(("fsq", gr - 2, 7))
                    O("S", lambda s, xb=xb, hoc=hoc, fb=fb: s.activation(
                        f_stage[xb][:, hoc, :], ps_f[fb][:, :], AF.Identity,
                        bias=battn_sb[:, hoc:hoc + 1], scale=1.0),
                      waits=ws, inc="sev", tag=("evf", gr, hoc))

                stfw = [("evf", gr, 7)]
                if gr >= 2:
                    stfw.append(("stf", gr - 2))
                O("Y", lambda y, gr=gr, xb=xb: y.dma_start(
                    ffm_d[:, :, gr * 512:(gr + 1) * 512], f_stage[xb][:, :, :]),
                  waits=stfw, inc="dff", tag=("stf", gr))

                for j4 in range(4):
                    jb = j4 % 2
                    jsem = "dta" if jb == 0 else "dtb"
                    for hoc in range(HC):
                        xtra = [("evf", gr, hoc)]
                        if hoc == 0:
                            k = gr * 2 + (j4 // 2)
                            if k >= 1:
                                xtra.append((jsem, 16 * k))
                        transpose_to(
                            (lambda xb=xb, hoc=hoc, j4=j4:
                             f_stage[xb][:, hoc, j4 * 128:(j4 + 1) * 128]),
                            (lambda jb=jb, hoc=hoc: ftm_stage[jb][:, hoc * 128:(hoc + 1) * 128]),
                            "V", xtra, (gr, j4, hoc, "ftm"))
                    O("Y", lambda y, gr=gr, j4=j4, jb=jb: y.dma_start(
                        ftm_d[:, gr * 4 + j4, :], ftm_stage[jb][:, :]),
                      waits=[("trev", gr, j4, 7, "ftm")], inc=jsem, tag=("stftm", gr, j4))
                w.ev[("fsfree", gr)] = w.ev[("tr", gr, 3, 7, "ftm")]

                if gr < OWN_GR:
                    for hoc in range(HC):
                        fq = hoc % 2
                        ws = [("evf", gr, hoc)]
                        if hoc >= 2:
                            ws.append(("mmq", gr, hoc - 2))
                        O("V", lambda v, xb=xb, hoc=hoc, fq=fq: v.tensor_mul(
                            fsq[fq][:, :], f_stage[xb][:, hoc, :], f_stage[xb][:, hoc, :]),
                          waits=ws, inc="vev", tag=("fsq", gr, hoc))
                        O("T", lambda t, fq=fq, hoc=hoc: t.matmul(
                            ps_nsq[0:1, :], ones_col[:, :], fsq[fq][:, :],
                            start=(hoc == 0), stop=(hoc == 7)),
                          waits=[("fsq", gr, hoc), "c_oc"] +
                                ([("evq", gr - 1)] if (hoc == 0 and gr >= 1) else []),
                          inc="tmm", tag=("mmq", gr, hoc))
                    O("V", lambda v, gr=gr: v.tensor_scalar(
                        out=negCf[0:1, gr * 512:(gr + 1) * 512], in0=ps_nsq[0:1, :],
                        scalar1=-1.0, scalar2=BSC * MAXD, op0=ALU.mult, op1=ALU.add),
                      waits=[("mmq", gr, 7)], inc="vev")
                    O("V", lambda v, gr=gr: v.tensor_scalar(
                        out=negCn[0:1, gr * 512:(gr + 1) * 512], in0=ps_nsq[0:1, :],
                        scalar1=-1.0, scalar2=None, op0=ALU.mult),
                      inc="vev", tag=("evq", gr))

                    for tt in range(4):
                        for hc in range(HC):
                            ws = [("trev", gr, tt, hc, "xn"), "ld_all"]
                            if hc == 0:
                                ws.append("ev_br" if prev_log[0] is None
                                          else ("evlog",) + prev_log[0])
                            O("T", lambda t, gr=gr, tt=tt, hc=hc: t.matmul(
                                ps_gate[:, :32],
                                xn_own[:, hc, gr * 512 + tt * 128: gr * 512 + (tt + 1) * 128],
                                gw_sb[:, hc, :], start=(hc == 0), stop=(hc == 7)),
                              waits=ws, inc=("tmm" if hc == 7 else None),
                              tag=(("mmg", gr, tt) if hc == 7 else None))
                        O("V", lambda v, gr=gr, tt=tt: v.tensor_copy(
                            logits_sb[:, gr * 4 + tt, :], ps_gate[:, :32]),
                          waits=[("mmg", gr, tt)], inc="vev", tag=("evlog", gr, tt))
                        prev_log[0] = (gr, tt)

                if gr == 3:
                    emit_routing()
                if gr == 5:
                    for t16 in range(16):
                        transpose_to((lambda t16=t16: rt_comb[:, t16, :]),
                                     (lambda t16=t16: combT[:, t16 * 128:(t16 + 1) * 128]),
                                     "V", [("comb", 15)], (t16, "comb"), prows=32)
                    emit_crow_load(w, 0, [("trev", 15, "comb")])

            if debug_scratch:
                O("Y", lambda y: y.dma_start(comb_dbg[:, :], combT[:, :]),
                  waits=[("trev", 15, "comb")], inc="dmisc")
                O("Y", lambda y: y.dma_start(negc_dbg[0:1, :], negCf[:, :]),
                  waits=[("evq", 3)], inc="dmisc")
                O("Y", lambda y: y.dma_start(negc_dbg[1:2, :], negCn[:, :]),
                  inc="dmisc")


        def ph2(w):
            O = w.op
            bw = w.bar(bar1)
            lastA = [None, None, None, None]
            gfree = [None, None, None, None]

            O("G", lambda g: g.memset(crep_one[:, :], 1.0),
              waits=bw, inc="gco", tag="c_cone")

            def emit_loads(e):
                emit_moe_loads(w, e, [("mmD", (e - 2) * NTB + NTB - 1, 1)])
                if e < 32:
                    emit_crow_load(w, e, [("crepmm", e - 1, 3)])

            def crep_mms(e):
                par = e % 2
                csem = "dca" if par == 0 else "dcb"
                cval = w.ev[("ldc", e)][1]
                for ic in range(4):
                    O("T", lambda t, ic=ic, par=par: t.matmul(
                        psA[:, ic * 512:(ic + 1) * 512], ones_row[:, :],
                        crow[par][:, ic * 512:(ic + 1) * 512], start=True, stop=True),
                      waits=[(csem, cval), lastA[ic]],
                      inc="tmm", tag=("crepmm_t", e, ic))
                    O("V", lambda v, ic=ic, par=par: v.tensor_copy(
                        crep_sb[par][:, ic, :], psA[:, ic * 512:(ic + 1) * 512]),
                      waits=[("crepmm_t", e, ic)], inc="vev", tag=("crepmm", e, ic))
                    lastA[ic] = ("crepmm", e, ic)

            def down_half(v_, h):
                e_, tb_ = v_ // NTB, v_ % NTB
                par = e_ % 2
                wsem = "dwa" if par == 0 else "dwb"
                wval = w.ev[("ldw", e_)][1]
                for hcq in range(4):
                    hcol = h * 4 + hcq
                    for ic in range(4):
                        ws = []
                        if hcq == 0 and ic == 0:
                            ws = [("zmul", v_, 3), (wsem, wval)]
                            if h == 0 and v_ >= 1:
                                ws.append(("add", v_ - 1, 1))
                            if h == 1:
                                ws.append(("add", v_, 0))
                        last = hcq == 3 and ic == 3
                        O("T", lambda t, par=par, ic=ic, hcol=hcol, hcq=hcq, v_=v_: t.matmul(
                            psB[:, hcq * 512:(hcq + 1) * 512],
                            wd_sb2[par][:, ic, hcol * 128:(hcol + 1) * 128],
                            z_sb[v_ % 2][:, ic, :], start=(ic == 0), stop=(ic == 3)),
                          waits=ws, inc=("tmm" if last else None),
                          tag=(("mmD", v_, h) if last else None))
                e0 = e_ == 0

                def addfn(v, v_=v_, h=h, tb_=tb_, e0=e0):
                    dst = acc_sb[:, tb_, h, :, :].rearrange("p a b -> p (a b)")
                    if e0:
                        return v.tensor_copy(dst, psB[:, :])
                    return v.tensor_tensor(dst, dst, psB[:, :], ALU.add)
                O("V", addfn, waits=[("mmD", v_, h)], inc="vev", tag=("add", v_, h))

            crep_mms(0)
            emit_crow_load(w, 1, [("crepmm", 0, 3)])

            for u in range(NU):
                e, tb = u // NTB, u % NTB
                par = e % 2
                zb = u % 2
                wsem = "dwa" if par == 0 else "dwb"
                wval = w.ev[("ldw", e)][1]

                if u >= 1:
                    down_half(u - 1, 0)
                if tb == 2 and (e + 1) < 32:
                    crep_mms(e + 1)

                for ic in range(4):
                    for hc in range(HC):
                        ws = []
                        if hc == 0:
                            ws = [lastA[ic], (wsem, wval)]
                            if u == 0 and ic == 0:
                                ws += list(bw)
                        O("T", lambda t, par=par, ic=ic, hc=hc, tb=tb: t.matmul(
                            psA[:, ic * 512:(ic + 1) * 512],
                            wg_sb2[par][:, hc, ic * 128:(ic + 1) * 128],
                            xn_own[:, hc, tb * 512:(tb + 1) * 512],
                            start=(hc == 0), stop=(hc == 7)),
                          waits=ws, inc=("tmm" if hc == 7 else None),
                          tag=(("mmGi", u, ic) if hc == 7 else None))
                    ws = [("mmGi", u, ic)]
                    if u >= 2:
                        ws.append(("zmul", u - 2, ic))
                    O("S", lambda s, zb=zb, ic=ic: s.activation(
                        g_sb[zb][:, ic, :], psA[:, ic * 512:(ic + 1) * 512], AF.Silu),
                      waits=ws, inc="sev", tag=("silu", u, ic))

                if u >= 1:
                    down_half(u - 1, 1)
                if tb == 0 and u >= 4 and (e + 1) < NE:
                    emit_loads(e + 1)

                for ic in range(4):
                    for hc in range(HC):
                        ws = [("silu", u, ic)] if hc == 0 else []
                        O("T", lambda t, par=par, ic=ic, hc=hc, tb=tb: t.matmul(
                            psA[:, ic * 512:(ic + 1) * 512],
                            wu_sb2[par][:, hc, ic * 128:(ic + 1) * 128],
                            xn_own[:, hc, tb * 512:(tb + 1) * 512],
                            start=(hc == 0), stop=(hc == 7)),
                          waits=ws, inc=("tmm" if hc == 7 else None),
                          tag=(("mmUi", u, ic) if hc == 7 else None))
                    if e >= 32:
                        crep_fn = (lambda tb=tb: crep_one[:, tb * 512:(tb + 1) * 512])
                    else:
                        crep_fn = (lambda par=par, tb=tb: crep_sb[par][:, tb, :])
                    ws = [("mmUi", u, ic)]
                    if e >= 32 and ic == 0:
                        ws.append("c_cone")
                    if u >= 2:
                        ws.append(("zmul", u - 2, ic))
                    O("V", lambda v, zb=zb, ic=ic, crep_fn=crep_fn: v.scalar_tensor_tensor(
                        out=u_sb[zb][:, ic, :], in0=psA[:, ic * 512:(ic + 1) * 512],
                        scalar=1.0, in1=crep_fn(), op0=ALU.bypass, op1=ALU.mult),
                      waits=ws, inc="vev", tag=("stt", u, ic))
                    lastA[ic] = ("stt", u, ic)
                for ic in range(4):
                    zw = [("stt", u, ic)]
                    if u >= 2 and ic == 0:
                        zw.append(("mmD", u - 2, 1))
                    O("V", lambda v, zb=zb, ic=ic: v.tensor_mul(
                        z_sb[zb][:, ic, :], g_sb[zb][:, ic, :], u_sb[zb][:, ic, :]),
                      waits=zw, inc="vev", tag=("zmul", u, ic), drain=(ic == 0))

            down_half(NU - 1, 0)
            down_half(NU - 1, 1)

            gtr = [0]
            for tb in range(NTB):
                ws = [("add", 33 * NTB + tb, 0), ("add", 33 * NTB + tb, 1)]
                if tb >= 1:
                    ws.append(("gsilfree", tb - 1))
                O("S", lambda s, tb=tb: s.activation(
                    g_sil[:, :, :, :], acc_sb[:, tb, :, :, :], AF.Silu),
                  waits=ws, inc="sev", tag=("gsil", tb))
                for tt4 in range(4):
                    tpar = (tb * 4 + tt4) % 2
                    for hcol in range(HC):
                        half, hcq = hcol // 4, hcol % 4
                        k = gtr[0] % 4
                        ws = [("gsil", tb), gfree[k]]
                        if gtr[0] == 0:
                            ws.append(("add", NU - 1, 1))
                        if hcol == 0 and (tb * 4 + tt4) >= 2:
                            ws.append(("dgw", 16 * (tb * 4 + tt4 - 1)))
                        O("T", lambda t, tb=tb, half=half, hcq=hcq, tt4=tt4, k=k: t.transpose(
                            psB[:, k * 512:k * 512 + 128],
                            g_sil[:, half, hcq, tt4 * 128:(tt4 + 1) * 128], ident_f32[:, :]),
                          waits=ws, inc="tmm", tag=("gtr", gtr[0]))
                        O("V", lambda v, tpar=tpar, hcol=hcol, k=k: v.tensor_copy(
                            gtm_stage[tpar][:, hcol * 128:(hcol + 1) * 128],
                            psB[:, k * 512:k * 512 + 128]),
                          waits=[("gtr", gtr[0])], inc="vev", tag=("gtrev", gtr[0]))
                        gfree[k] = ("gtrev", gtr[0])
                        gtr[0] += 1
                    O("Y", lambda y, tb=tb, tt4=tt4, tpar=tpar: y.dma_start(
                        gscr_d[:, tb * 4 + tt4, :], gtm_stage[tpar][:, :]),
                      waits=[("gtrev", gtr[0] - 1)], inc="dgw", tag=("stg", tb * 4 + tt4))
                w.ev[("gsilfree", tb)] = w.ev[("gtr", gtr[0] - 1)]

        def ph3(w):
            O = w.op
            bw = w.bar(bar2)
            O("Y", lambda y: y.dma_start(f_fm[:, :, 0:256], ffm_d[:, :, 0:256]),
              waits=bw, inc="dffr", tag="ldffm0")
            O("Y", lambda y: y.dma_start(f_fm[:, :, 256:], ffm_d[:, :, 256:]),
              inc="dffr", tag="ldffm")

            def emit_band_load(st):
                sb_ = st % 2
                bsem = "dba" if sb_ == 0 else "dbb"
                bandw = list(bw) if st < 2 else [("bandlast", st - 2)]
                O("G", lambda g, st=st, sb_=sb_: g.dma_start(
                    bands_sb[sb_][:, :, :], bands_d[st, :, :, :]),
                  waits=bandw, inc=bsem, tag=("ldband", st))

            def emit_g_load(st):
                sb_ = st % 2
                gsem = "dca" if sb_ == 0 else "dcb"
                gw_ = list(bw) if st < 2 else [("fin", st - 2, 1)]
                O("Y", lambda y, st=st, sb_=sb_: y.dma_start(
                    gbuf[sb_][:, :, :], gscr_d[:, st * 2:st * 2 + 2, :]),
                  waits=gw_, inc=gsem, tag=("ldg", st))

            def emit_x_load(st):
                xw_ = list(bw) if st < 1 else [("fin", st - 1, 1)]
                O("Y", lambda y, st=st: y.dma_start(
                    xbuf3[:, :, :],
                    x_d[st * 256:(st + 1) * 256, :].rearrange("(a p) d -> p a d", p=128)),
                  waits=xw_, inc="dxr", tag=("ldx3", st))

            def emit_pv(st, jc):
                pb = jc % 2
                ch, cpar = jc // 4, (jc // 4) % 2
                csem = "dra" if cpar == 0 else "drb"
                pvw = [("exp", st, jc), (csem, w.ev[("ldftm", st, ch)][1])]
                if jc == 0 and st >= 1:
                    pvw.append(("fin", st - 1, 1))
                first, last = jc == 0, jc == NKB - 1
                for q in range(2):
                    for nh in range(2):
                        O("T", lambda t, q=q, nh=nh, pb=pb, cpar=cpar, jc=jc,
                                 first=first, last=last: t.matmul(
                            psAcc[q][:, nh * 512:(nh + 1) * 512],
                            pt_sb[pb][:, q * 128:(q + 1) * 128],
                            ftm_buf[cpar][:, jc % 4, nh * 512:(nh + 1) * 512],
                            start=first, stop=last, skip_group_check=True),
                          waits=pvw if (q == 0 and nh == 0) else [], inc=None)
                    O("T", lambda t, q=q, pb=pb, first=first, last=last: t.matmul(
                        psRS[q][:, 0:1], pt_sb[pb][:, q * 128:(q + 1) * 128],
                        ones_col[:, :], start=first, stop=last, skip_group_check=True),
                      waits=[], inc=("tmm" if q == 1 else None),
                      tag=(("pv", st, jc) if q == 1 else None))

            def emit_finale(st):
                sb_ = st % 2
                O("V", lambda v: v.tensor_copy(rs_sb[:, 0:1], psRS[0][:, 0:1]),
                  waits=[("pv", st, NKB - 1)], inc="vev")
                O("V", lambda v: v.tensor_copy(rs_sb[:, 1:2], psRS[1][:, 0:1]), inc="vev")
                O("V", lambda v: v.reciprocal(rec_sb[:, :], rs_sb[:, :]),
                  inc="vev", drain=True)
                for q in range(2):
                    ws = [("ldg", st)] if q == 0 else []
                    O("V", lambda v, q=q, sb_=sb_: v.scalar_tensor_tensor(
                        out=sg_sb[:, q, :], in0=psAcc[q][:, :], scalar=rec_sb[:, q:q + 1],
                        in1=gbuf[sb_][:, q, :], op0=ALU.mult, op1=ALU.mult),
                      waits=ws, inc="vev", drain=(q == 0))
                for q in range(2):
                    ws = [("dxr", w.ev[("ldx3", st)][1])] if q == 0 else []
                    O("V", lambda v, q=q: v.tensor_add(sg_sb[:, q, :], sg_sb[:, q, :],
                                                       xbuf3[:, q, :]),
                      waits=ws, inc="vev", tag=("fin", st, q))
                O("Y", lambda y, st=st: y.dma_start(
                    out_d[st * 256:(st + 1) * 256, :].rearrange("(a p) d -> p a d", p=128),
                    sg_sb[:, :, :]),
                  waits=[("fin", st, 1)], inc="dor", tag=("stout", st))

            emit_band_load(0)
            emit_band_load(1)
            emit_g_load(0)
            emit_g_load(1)
            emit_x_load(0)
            prev = None
            for st in range(NST):
                sb_ = st % 2
                bsem = "dba" if sb_ == 0 else "dbb"
                bkbs = BAND_KBS[st]
                for jc in range(NKB):
                    pb = jc % 2
                    is_band = jc in bkbs
                    negc = negCn if is_band else negCf
                    for hc in range(HC):
                        ws = []
                        if hc == 0:
                            ws = [("dffr", 16 if (st == 0 and jc < 2) else 32)]
                            if st > 0 or jc >= 2:
                                pst, pjc = (st, jc - 2) if jc >= 2 else (st - 1, NKB - 2 + jc)
                                ws.append(("exp", pst, pjc))
                        O("T", lambda t, pb=pb, hc=hc, jc=jc, st=st: t.matmul(
                            psS[pb][:, :256], f_fm[:, hc, jc * 128:(jc + 1) * 128],
                            f_fm[:, hc, st * 256:(st + 1) * 256],
                            start=(hc == 0), stop=False, skip_group_check=True),
                          waits=ws, inc=None)
                    O("T", lambda t, pb=pb, negc=negc, st=st: t.matmul(
                        psS[pb][:, :256], ones_row[:, :], negc[0:1, st * 256:(st + 1) * 256],
                        start=False, stop=True, skip_group_check=True),
                      waits=[], inc="tmm", tag=("st", st, jc))
                    expw = [("st", st, jc)]
                    if is_band:
                        i = bkbs.index(jc)
                        O("V", lambda v, pb=pb, sb_=sb_, i=i: v.tensor_tensor(
                            psS[pb][:, :256], psS[pb][:, :256], bands_sb[sb_][:, i, :],
                            ALU.add),
                          waits=[("st", st, jc), (bsem, w.ev[("ldband", st)][1])],
                          inc="vev", tag=("bandadd", st, i))
                        expw.append(("bandadd", st, i))
                        w.ev[("bandlast", st)] = w.ev[("bandadd", st, i)]
                    if st > 0 or jc >= 2:
                        pst, pjc = (st, jc - 2) if jc >= 2 else (st - 1, NKB - 2 + jc)
                        expw.append(("pv", pst, pjc))
                    O("S", lambda s, pb=pb: s.activation(pt_sb[pb][:, :], psS[pb][:, :256],
                                                         AF.Exp),
                      waits=expw, inc="sev", tag=("exp", st, jc))
                    if prev is not None:
                        emit_pv(*prev)
                    prev = (st, jc)
                    if jc % 4 == 0:
                        ch = jc // 4
                        cpar = ch % 2
                        csem = "dra" if cpar == 0 else "drb"
                        gch = st * 16 + ch
                        if gch < 2:
                            cw = [("dffr", 32)]
                        else:
                            pst, pc = (gch - 2) // 16, (gch - 2) % 16
                            cw = [("pv", pst, pc * 4 + 3)]
                        O("Y", lambda y, st=st, ch=ch, cpar=cpar: y.dma_start(
                            ftm_buf[cpar][:, :, :], ftm_d[:, ch * 4:(ch + 1) * 4, :]),
                          waits=cw, inc=csem, tag=("ldftm", st, ch))
                    if jc == 0 and st >= 1:
                        emit_finale(st - 1)
                        emit_x_load(st)
                        if st + 1 < NST:
                            emit_band_load(st + 1)
                            emit_g_load(st + 1)
            emit_pv(NST - 1, NKB - 1)
            emit_finale(NST - 1)

            O("Y", None, waits=[("dor", w.c["dor"])])

        ph12s = top.enter_context(ExitStack())
        xn_own = ph12s.enter_context(nc.sbuf_tensor("xn_own", [128, HC, TOK], BF16))
        combT = ph12s.enter_context(nc.sbuf_tensor("combT", [32, TOK], BF16))
        wg_sb2 = [ph12s.enter_context(nc.sbuf_tensor(f"wg_sb2{i}", [128, HC, NI], BF16))
                  for i in range(2)]
        wu_sb2 = [ph12s.enter_context(nc.sbuf_tensor(f"wu_sb2{i}", [128, HC, NI], BF16))
                  for i in range(2)]
        wd_sb2 = [ph12s.enter_context(nc.sbuf_tensor(f"wd_sb2{i}", [128, 4, D], BF16))
                  for i in range(2)]
        crow = [ph12s.enter_context(nc.sbuf_tensor("crow0", [1, TOK], BF16))]
        crow.append(crow[0])
        with ExitStack() as ph1s:
            P = ph1s.enter_context
            gb_bf = P(nc.sbuf_tensor("gb_bf", [1, 32], BF16))
            wt_sb = P(nc.sbuf_tensor("wt_sb", [128, HC, D], BF16))
            x_stage = [P(nc.sbuf_tensor(f"x_stage{i}", [128, 4, D], F32)) for i in range(2)]
            xn_stage = [P(nc.sbuf_tensor(f"xn_stage{i}", [128, 4, D], BF16)) for i in range(2)]
            xnf_stage = [P(nc.sbuf_tensor(f"xnf_stage{i}", [128, HC, 512], BF16))
                         for i in range(2)]
            f_stage = [P(nc.sbuf_tensor(f"f_stage{i}", [128, HC, 512], BF16)) for i in range(2)]
            ftm_stage = [P(nc.sbuf_tensor(f"ftm_stage{i}", [128, D], BF16)) for i in range(2)]
            fsq = [P(nc.sbuf_tensor(f"fsq{i}", [128, 512], BF16)) for i in range(2)]
            stat_sum = P(nc.sbuf_tensor("stat_sum", [128, 4], F32))
            stat_sumsq = P(nc.sbuf_tensor("stat_sumsq", [128, 4], F32))
            stat_mean = P(nc.sbuf_tensor("stat_mean", [128, 4], F32))
            stat_var = P(nc.sbuf_tensor("stat_var", [128, 4], F32))
            stat_msq = P(nc.sbuf_tensor("stat_msq", [128, 4], F32))
            stat_ln = P(nc.sbuf_tensor("stat_ln", [128, 4], F32))
            stat_rstd = [P(nc.sbuf_tensor(f"stat_rstd{i}", [128, 4], F32)) for i in range(2)]
            stat_nmr = [P(nc.sbuf_tensor(f"stat_nmr{i}", [128, 4], F32)) for i in range(2)]
            logits_sb = P(nc.sbuf_tensor("logits_sb", [128, 16, 32], F32))
            scores_sb = P(nc.sbuf_tensor("scores_sb", [128, 16, 32], F32))
            biasrep = P(nc.sbuf_tensor("biasrep", [128, 32], F32))
            rt_sc = P(nc.sbuf_tensor("rt_sc", [128, 32], F32))
            rt_pair = P(nc.sbuf_tensor("rt_pair", [128, 48], F32))
            rt_m1 = P(nc.sbuf_tensor("rt_m1", [128, 8], F32))
            rt_m2 = P(nc.sbuf_tensor("rt_m2", [128, 8], F32))
            rt_m3 = P(nc.sbuf_tensor("rt_m3", [128, 8], F32))
            rt_gs = P(nc.sbuf_tensor("rt_gs", [128, 8], F32))
            rt_gs8 = P(nc.sbuf_tensor("rt_gs8", [128, 8], F32))
            rt_gmask = P(nc.sbuf_tensor("rt_gmask", [128, 8], F32))
            rt_smask = P(nc.sbuf_tensor("rt_smask", [128, 32], F32))
            rt_masked = P(nc.sbuf_tensor("rt_masked", [128, 32], F32))
            rt_tk8 = P(nc.sbuf_tensor("rt_tk8", [128, 8], F32))
            rt_emask = P(nc.sbuf_tensor("rt_emask", [128, 32], F32))
            rt_w8 = P(nc.sbuf_tensor("rt_w8", [128, 32], F32))
            rt_wsum = P(nc.sbuf_tensor("rt_wsum", [128, 1], F32))
            rt_rec = P(nc.sbuf_tensor("rt_rec", [128, 1], F32))
            rt_comb = P(nc.sbuf_tensor("rt_comb", [128, 16, 32], BF16))
            ps_f = [P(nc.psum_tensor(f"ps_f{i}", [128, 512], F32)) for i in range(2)]
            ps_tr = [P(nc.psum_tensor(f"ps_tr{i}", [128, 1024], BF16)) for i in range(4)]
            ps_nsq = P(nc.psum_tensor("ps_nsq", [128, 512], F32))
            ps_gate = P(nc.psum_tensor("ps_gate", [128, 512], F32))

            def ph1_run(w):
                ph1(w)
                bar1.update(w.snap())
                evc1.update(w.ev)

            with nc.Block() as blk:
                walks(blk, ph1_run)

        if phases >= 2:
            with ExitStack() as ph2s:
                P = ph2s.enter_context
                crep_sb = [P(nc.sbuf_tensor(f"crep_sb{i}", [128, 4, 512], BF16))
                           for i in range(2)]
                crep_one = P(nc.sbuf_tensor("crep_one", [128, TOK], BF16))
                g_sb = [P(nc.sbuf_tensor(f"g_sb{i}", [128, 4, 512], BF16)) for i in range(2)]
                u_sb = [P(nc.sbuf_tensor(f"u_sb{i}", [128, 4, 512], BF16)) for i in range(2)]
                z_sb = [P(nc.sbuf_tensor(f"z_sb{i}", [128, 4, 512], BF16)) for i in range(2)]
                acc_sb = P(nc.sbuf_tensor("acc_sb", [128, NTB, 2, 4, 512], BF16))
                g_sil = P(nc.sbuf_tensor("g_sil", [128, 2, 4, 512], F32))
                gtm_stage = [P(nc.sbuf_tensor(f"gtm_stage{i}", [128, D], BF16))
                             for i in range(2)]
                psA = P(nc.psum_tensor("psA", [128, 2048], F32))
                psB = P(nc.psum_tensor("psB", [128, 2048], F32))

                def ph2_run(w):
                    ph2(w)
                    bar2.update(w.snap())

                with nc.Block() as blk:
                    walks(blk, ph2_run, init=bar1, init_ev=evc1)

        ph12s.close()
        if phases >= 3:
            with ExitStack() as ph3s:
                P = ph3s.enter_context
                f_fm = P(nc.sbuf_tensor("f_fm", [128, HC, S], BF16))
                ftm_buf = [P(nc.sbuf_tensor(f"ftm_buf{i}", [128, 4, D], BF16))
                           for i in range(2)]
                pt_sb = [P(nc.sbuf_tensor(f"pt_sb{i}", [128, 256], BF16)) for i in range(2)]
                bands_sb = [P(nc.sbuf_tensor(f"bands_sb{i}", [128, 4, 256], BF16))
                            for i in range(2)]
                gbuf = [P(nc.sbuf_tensor(f"gbuf{i}", [128, 2, D], BF16)) for i in range(2)]
                xbuf3 = P(nc.sbuf_tensor("xbuf3", [128, 2, D], F32))
                sg_sb = P(nc.sbuf_tensor("sg_sb", [128, 2, D], F32))
                rs_sb = P(nc.sbuf_tensor("rs_sb", [128, 2], F32))
                rec_sb = P(nc.sbuf_tensor("rec_sb", [128, 2], F32))
                rsdbg_sb = P(nc.sbuf_tensor("rsdbg_sb", [128, 4], F32))
                psS = [P(nc.psum_tensor(f"psS{i}", [128, 512], F32)) for i in range(2)]
                psAcc = [P(nc.psum_tensor(f"psAcc{i}", [128, 1024], F32)) for i in range(2)]
                psRS = [P(nc.psum_tensor(f"psRS{i}", [128, 512], F32)) for i in range(2)]

                with nc.Block() as blk:
                    walks(blk, ph3, init=bar2)

    return nc



def _build_bands(off):
    bands = np.zeros((NST, 128, 4, 256), np.float32)
    for st in range(NST):
        for i, kb in enumerate(BAND_KBS[st]):
            kk = np.arange(128)
            qq = np.arange(256)
            j_true = (kb * 128 + kk + off) % S
            q_true = st * 256 + qq + off
            d = np.abs(q_true[None, :] - j_true[:, None]).astype(np.float32)
            bands[st, :, i, :] = BSC * np.minimum(d, float(MAXD))
    return bands


def prep_inputs(L0_attn_x, L0_attn_W, L0_attn_b, L0_ffn_gate_w, L0_ffn_gate_bias,
                L0_ffn_w_gate, L0_ffn_w_up, L0_ffn_w_down, L0_ffn_sh_gate,
                L0_ffn_sh_up, L0_ffn_sh_down):
    x = np.asarray(L0_attn_x, np.float32)
    wt = np.ascontiguousarray(np.asarray(L0_attn_W, np.float32).T)
    battn = np.ascontiguousarray(np.asarray(L0_attn_b, np.float32).reshape(D, 1))
    gw = np.ascontiguousarray(np.asarray(L0_ffn_gate_w, np.float32).T)
    gb = np.ascontiguousarray(np.asarray(L0_ffn_gate_bias, np.float32).reshape(1, 32))

    wg = np.asarray(L0_ffn_w_gate, np.float32)
    wu = np.asarray(L0_ffn_w_up, np.float32)
    wd = np.asarray(L0_ffn_w_down, np.float32)
    shg = np.asarray(L0_ffn_sh_gate, np.float32)
    shu = np.asarray(L0_ffn_sh_up, np.float32)
    shd = np.asarray(L0_ffn_sh_down, np.float32)

    moe = np.empty((NE, 3, NI * D), np.float32)
    for e in range(32):
        moe[e, 0] = wg[e].T.reshape(-1)
        moe[e, 1] = wu[e].T.reshape(-1)
        moe[e, 2] = wd[e].T.reshape(-1)
    shdT = np.ascontiguousarray(shd.T)
    for k in range(2):
        e = 32 + k
        moe[e, 0] = shg[k * NI:(k + 1) * NI, :].T.reshape(-1)
        moe[e, 1] = shu[k * NI:(k + 1) * NI, :].T.reshape(-1)
        moe[e, 2] = shdT[k * NI:(k + 1) * NI, :].reshape(-1)

    bands_cache = {}
    maps = []
    for c in range(8):
        b, off = c // 4, (c % 4) * TOK
        if off not in bands_cache:
            bands_cache[off] = _build_bands(off)
        xr = np.ascontiguousarray(np.roll(x[b], -off, axis=0))
        maps.append(dict(x=xr, wt=wt, battn=battn, gw=gw, gb=gb, moe=moe,
                         bands=bands_cache[off]))
    return maps


def kernel(**inputs):
    in_maps = prep_inputs(**inputs)
    nc = build_program()
    res = run_bass_kernel_spmd(nc, in_maps, core_ids=list(range(8)))
    out = np.empty((B, S, D), np.float32)
    for c in range(8):
        b, off = c // 4, (c % 4) * TOK
        out[b, off:off + TOK, :] = res.results[c]["out"]
    return out


# version 06 (final)
# speedup vs baseline: 1.2627x; 1.2627x over previous
import numpy as np
from contextlib import ExitStack

import concourse.bass as bass
import concourse.mybir as mybir
from concourse.bass_utils import run_bass_kernel_spmd

F32 = mybir.dt.float32
BF16 = mybir.dt.bfloat16
AF = mybir.ActivationFunctionType
ALU = mybir.AluOpType
AX = mybir.AxisListType

B, S, D = 2, 8192, 1024
HC = 8
TOK = 2048
NGR = 16
OWN_GR = 4
NE = 34
NI = 512
NTB = 4
NU = NE * NTB
NST = 8
NKB = 64
MAXD = 10
BSC = 0.01

ENGINE_OWN_SEM = {"T": "tmm", "V": "vev", "S": "sev", "G": "gco"}
DMA_SEMS = ("dxa", "dxb", "dff", "dffr", "dta", "dtb", "dra", "drb",
            "dwa", "dwb", "dca", "dcb", "dba", "dbb", "dgr", "dxr",
            "dor", "dmisc", "dgw")
CORE_SEMS = ("tmm", "vev", "sev", "gco")
ALL_SEMS = CORE_SEMS + DMA_SEMS


class Walk:

    def __init__(self, key, eng, sems):
        self.key = key
        self.eng = eng
        self.sems = sems
        self.c = {s: 0 for s in ALL_SEMS}
        self.ev = {}
        self.waited = {s: 0 for s in ALL_SEMS}

    def snap(self):
        return dict(self.c)

    def bar(self, snap):
        return [(s, v) for s, v in snap.items() if v > 0]

    def op(self, key, fn, waits=(), inc=None, inc_amt=None, tag=None, drain=False):
        post = None
        if inc is not None:
            amt = inc_amt if inc_amt is not None else (16 if inc in DMA_SEMS else 1)
            self.c[inc] += amt
            post = self.c[inc]
            if tag is not None:
                self.ev[tag] = (inc, post)
        if self.key == key:
            own = ENGINE_OWN_SEM.get(self.key)
            for wv in waits:
                if wv is None:
                    continue
                if isinstance(wv, tuple) and isinstance(wv[0], str) and wv[0] in ALL_SEMS:
                    s, v = wv
                else:
                    s, v = self.ev[wv]
                if s == own:
                    continue
                if v <= self.waited[s]:
                    continue
                self.waited[s] = v
                self.eng.wait_ge(self.sems[s], v)
            if fn is not None:
                if drain:
                    self.eng.drain()
                ins = fn(self.eng)
                if inc is not None:
                    ins.then_inc(self.sems[inc], amt)
        return post


def _band_instances():
    out = []
    for st in range(NST):
        insts = []
        for rel in (128, 0, -128, -256):
            kb = (st * 256 - rel) // 128
            if kb < 0:
                kb = NKB - 1
            insts.append(kb)
        out.append(insts)
    return out


BAND_KBS = _band_instances()


def build_program(phases=3, debug_scratch=False):
    nc = bass.Bass("TRN2", target_bir_lowering=False)

    x_d = nc.dram_tensor("x", [S, D], F32, kind="ExternalInput")
    wt_d = nc.dram_tensor("wt", [D, D], F32, kind="ExternalInput")
    battn_d = nc.dram_tensor("battn", [D, 1], F32, kind="ExternalInput")
    gw_d = nc.dram_tensor("gw", [D, 32], F32, kind="ExternalInput")
    gb_d = nc.dram_tensor("gb", [1, 32], F32, kind="ExternalInput")
    moe_d = nc.dram_tensor("moe", [NE, 3, NI * D], F32, kind="ExternalInput")
    bands_d = nc.dram_tensor("bands", [NST, 128, 4, 256], F32, kind="ExternalInput")
    out_d = nc.dram_tensor("out", [TOK, D], F32, kind="ExternalOutput")

    skind = "ExternalOutput" if debug_scratch else "Internal"
    ffm_d = nc.dram_tensor("ffm", [128, HC, S], BF16, kind=skind)
    ftm_d = nc.dram_tensor("ftm", [128, NKB, D], BF16, kind=skind)
    gscr_d = nc.dram_tensor("gscr", [128, 16, D], BF16, kind=skind)
    comb_dbg = None
    if debug_scratch:
        comb_dbg = nc.dram_tensor("combdbg", [32, TOK], BF16, kind="ExternalOutput")
        negc_dbg = nc.dram_tensor("negcdbg", [2, TOK], BF16, kind="ExternalOutput")
        rs_dbg = nc.dram_tensor("rsdbg", [NST, 128, 4], F32, kind="ExternalOutput")

    with ExitStack() as top:
        E = top.enter_context
        sems = {s: E(nc.semaphore(s)) for s in ALL_SEMS}

        ones_row = E(nc.sbuf_tensor("ones_row", [1, 128], BF16))
        ones_col = E(nc.sbuf_tensor("ones_col", [128, 1], BF16))
        ident_bf = E(nc.sbuf_tensor("ident_bf", [128, 128], BF16))
        ident_f32 = E(nc.sbuf_tensor("ident_f32", [128, 128], F32))
        negCf = E(nc.sbuf_tensor("negCf", [1, TOK], BF16))
        negCn = E(nc.sbuf_tensor("negCn", [1, TOK], BF16))
        battn_sb = E(nc.sbuf_tensor("battn_sb", [128, HC], F32))
        gw_sb = E(nc.sbuf_tensor("gw_sb", [128, HC, 32], BF16))

        def walks(block, fn, init=None, init_ev=None):
            def run(key):
                def _f(eng):
                    w = Walk(key, eng, sems)
                    if init:
                        w.c.update(init)
                    if init_ev:
                        w.ev.update(init_ev)
                    fn(w)
                return _f
            block.tensor(run("T"))
            block.vector(run("V"))
            block.scalar(run("S"))
            block.sync(run("Y"))
            block.gpsimd(run("G"))

        bar1 = {}
        bar2 = {}
        evc1 = {}

        def emit_moe_loads(w, e, first_waits):
            O = w.op
            par = e % 2
            wsem = "dwa" if par == 0 else "dwb"
            srcs = [
                lambda e=e: moe_d[e, 0, :].rearrange("(hc p i) -> p hc i", p=128, i=NI),
                lambda e=e: moe_d[e, 1, :].rearrange("(hc p i) -> p hc i", p=128, i=NI),
                lambda e=e: moe_d[e, 2, :].rearrange("(ic p h) -> p ic h", p=128, h=D),
            ]
            for m, dst in ((0, wg_sb2), (1, wu_sb2), (2, wd_sb2)):
                O("G", lambda g, dst=dst, par=par, src=srcs[m]: g.dma_start(
                    dst[par][:, :, :], src()),
                  waits=first_waits if m == 0 else (), inc=wsem,
                  tag=(("ldw", e) if m == 2 else None))

        def emit_crow_load(w, e, waits):
            O = w.op
            par = e % 2
            csem = "dca" if par == 0 else "dcb"
            O("Y", lambda y, e=e, par=par: y.dma_start(crow[par][:, :], combT[e:e + 1, :]),
              waits=waits, inc=csem, tag=("ldc", e))

        def ph1(w):
            O = w.op
            emit_moe_loads(w, 0, [])
            emit_moe_loads(w, 1, [])
            O("G", lambda g: g.memset(ones_row[:, :], 1.0), inc="gco", tag="c_or")
            O("G", lambda g: g.memset(ones_col[:, :], 1.0), inc="gco", tag="c_oc")
            O("G", lambda g: g.iota(x_stage[0][:, 0, 0:128], [[1, 128]], channel_multiplier=0,
                                    allow_small_or_imprecise_dtypes=True), inc="gco", tag="c_i1")
            O("G", lambda g: g.iota(x_stage[0][:, 1, 0:128], [[0, 128]], channel_multiplier=1,
                                    allow_small_or_imprecise_dtypes=True), inc="gco", tag="c_i2")
            O("V", lambda v: v.tensor_tensor(ident_bf[:, :], x_stage[0][:, 0, 0:128],
                                             x_stage[0][:, 1, 0:128], ALU.is_equal),
              waits=["c_i1", "c_i2"], inc="vev", tag="c_idb")
            O("V", lambda v: v.tensor_tensor(ident_f32[:, :], x_stage[0][:, 0, 0:128],
                                             x_stage[0][:, 1, 0:128], ALU.is_equal),
              inc="vev", tag="c_idf")
            O("G", lambda g: g.dma_start(wt_sb[:, :, :],
                                         wt_d[:, :].rearrange("(hc p) ho -> p hc ho", p=128)),
              inc="dmisc", tag="ld_wt")
            O("G", lambda g: g.dma_start(gw_sb[:, :, :],
                                         gw_d[:, :].rearrange("(hc p) e -> p hc e", p=128)),
              inc="dmisc", tag="ld_gw")
            O("G", lambda g: g.dma_start(gb_bf[:, :], gb_d[:, :]), inc="dmisc", tag="ld_gb")
            O("Y", lambda y: y.dma_start(battn_sb[:, :],
                                         battn_d[:, :].rearrange("(hc p) o -> p (hc o)", p=128),
                                         allow_slow_non_contiguous=True),
              inc="dmisc", tag="ld_ba")
            w.ev["ld_all"] = ("dmisc", w.c["dmisc"])
            O("T", lambda t: t.matmul(ps_gate[:, :32], ones_row[:, :], gb_bf[:, :],
                                      start=True, stop=True),
              waits=["ld_all", "c_or"], inc="tmm", tag="mm_br")
            O("V", lambda v: v.tensor_copy(biasrep[:, :], ps_gate[:, :32]),
              waits=["mm_br"], inc="vev", tag="ev_br")

            trcnt = [0]
            trfree = [None, None, None, None]

            def transpose_to(src_fn, dst_fn, dst_eng, extra_waits, tagbase, prows=128):
                b = trcnt[0] % 4
                O("T", lambda t, b=b, prows=prows: t.transpose(
                    ps_tr[b][0:prows, :128], src_fn(), ident_bf[:, :]),
                  waits=list(extra_waits) + ["c_idb", trfree[b]],
                  inc="tmm", tag=("tr",) + tagbase)
                if dst_eng == "S":
                    O("S", lambda s, b=b, prows=prows: s.activation(
                        dst_fn(), ps_tr[b][0:prows, :128], AF.Copy),
                      waits=[("tr",) + tagbase], inc="sev", tag=("trev",) + tagbase)
                else:
                    O("V", lambda v, b=b, prows=prows: v.tensor_copy(
                        dst_fn(), ps_tr[b][0:prows, :128]),
                      waits=[("tr",) + tagbase], inc="vev", tag=("trev",) + tagbase)
                trfree[b] = ("trev",) + tagbase
                trcnt[0] += 1

            prev_log = [None]

            def emit_routing():
                O = w.op
                O("S", lambda s: s.activation(scores_sb[:, :, :], logits_sb[:, :, :], AF.Sigmoid),
                  waits=[("evlog", 3, 3)], inc="sev", tag="s_scores")
                PAIRS = [(0, 1), (0, 2), (0, 3), (1, 2), (1, 3), (2, 3)]
                for t16 in range(16):
                    O("V", lambda v, t16=t16: v.tensor_add(rt_sc[:, :], scores_sb[:, t16, :],
                                                           biasrep[:, :]),
                      waits=["s_scores", "ev_br"], inc="vev")
                    for i, (a, b) in enumerate(PAIRS):
                        O("V", lambda v, i=i, a=a, b=b: v.tensor_add(
                            rt_pair[:, i * 8:(i + 1) * 8], rt_sc[:, a:32:4], rt_sc[:, b:32:4]),
                          inc="vev", drain=(i == 0))
                    O("V", lambda v: v.tensor_max(rt_m1[:, :], rt_pair[:, 0:8], rt_pair[:, 8:16]),
                      inc="vev")
                    O("V", lambda v: v.tensor_max(rt_m2[:, :], rt_pair[:, 16:24], rt_pair[:, 24:32]),
                      inc="vev")
                    O("V", lambda v: v.tensor_max(rt_m3[:, :], rt_pair[:, 32:40], rt_pair[:, 40:48]),
                      inc="vev")
                    O("V", lambda v: v.tensor_max(rt_m1[:, :], rt_m1[:, :], rt_m2[:, :]),
                      inc="vev", drain=True)
                    O("V", lambda v: v.tensor_max(rt_gs[:, :], rt_m1[:, :], rt_m3[:, :]),
                      inc="vev", drain=True)
                    O("V", lambda v: v.max(out=rt_gs8[:, :], in_=rt_gs[:, :]),
                      inc="vev", drain=True)
                    O("V", lambda v: v.tensor_scalar(out=rt_gmask[:, :], in0=rt_gs[:, :],
                                                     scalar1=rt_gs8[:, 3:4], scalar2=None,
                                                     op0=ALU.is_ge), inc="vev", drain=True)
                    O("V", lambda v: v.tensor_copy(
                        rt_smask[:, :].rearrange("p (g j) -> p g j", j=4),
                        rt_gmask[:, :, None].broadcast_to([128, 8, 4])), inc="vev", drain=True)
                    O("V", lambda v: v.tensor_mul(rt_masked[:, :], rt_sc[:, :], rt_smask[:, :]),
                      inc="vev", drain=True)
                    O("V", lambda v: v.max(out=rt_tk8[:, :], in_=rt_masked[:, :]),
                      inc="vev", drain=True)
                    O("V", lambda v: v.tensor_scalar(out=rt_emask[:, :], in0=rt_masked[:, :],
                                                     scalar1=rt_tk8[:, 7:8], scalar2=None,
                                                     op0=ALU.is_ge), inc="vev", drain=True)
                    O("V", lambda v, t16=t16: v.tensor_mul(rt_w8[:, :], scores_sb[:, t16, :],
                                                           rt_emask[:, :]), inc="vev", drain=True)
                    O("V", lambda v: v.reduce_sum(rt_wsum[:, :], rt_w8[:, :], axis=AX.X),
                      inc="vev", drain=True)
                    O("V", lambda v: v.tensor_scalar(out=rt_wsum[:, :], in0=rt_wsum[:, :],
                                                     scalar1=1e-20, scalar2=None, op0=ALU.add),
                      inc="vev", drain=True)
                    O("V", lambda v: v.reciprocal(rt_rec[:, :], rt_wsum[:, :]),
                      inc="vev", drain=True)
                    O("V", lambda v, t16=t16: v.tensor_scalar(
                        out=rt_comb[:, t16, :], in0=rt_w8[:, :],
                        scalar1=rt_rec[:, :], scalar2=2.5,
                        op0=ALU.mult, op1=ALU.mult),
                      inc="vev", tag=("comb", t16), drain=True)
            def emit_ldx(gr):
                xb = gr % 2
                xsem = "dxa" if xb == 0 else "dxb"
                xw = [("xn", gr - 2, 3)] if gr >= 2 else (["c_idf"] if gr == 0 else [])
                O("Y", lambda y, gr=gr, xb=xb: y.dma_start(
                    x_stage[xb][:, :, :],
                    x_d[gr * 512:(gr + 1) * 512, :].rearrange("(a p) d -> p a d", p=128)),
                  waits=xw, inc=xsem, tag=("ldx", gr))

            def emit_stats(gr):
                xb = gr % 2
                xsem = "dxa" if xb == 0 else "dxb"
                sp = gr % 2
                ldxw = [(xsem, w.ev[("ldx", gr)][1])]
                O("V", lambda v, xb=xb: v.reduce_sum(
                    stat_sum[:, :, None], x_stage[xb][:, :, :], axis=AX.X),
                  waits=ldxw, inc="vev")
                sqw = [("tr", gr - 2, 3, 7, "xn")] if gr >= 2 else []
                for tt in range(4):
                    O("V", lambda v, xb=xb, tt=tt: v.tensor_mul(
                        xn_stage[xb][:, 3, :], x_stage[xb][:, tt, :], x_stage[xb][:, tt, :]),
                      waits=sqw if tt == 0 else [], inc="vev")
                    O("V", lambda v, tt=tt, xb=xb: v.reduce_sum(
                        stat_sumsq[:, tt:tt + 1], xn_stage[xb][:, 3, :], axis=AX.X),
                      inc="vev", drain=True)
                O("V", lambda v: v.tensor_scalar(out=stat_mean[:, :], in0=stat_sum[:, :],
                                                 scalar1=1.0 / D, scalar2=None, op0=ALU.mult),
                  inc="vev", drain=True)
                O("V", lambda v: v.tensor_scalar(out=stat_msq[:, :], in0=stat_sumsq[:, :],
                                                 scalar1=1.0 / D, scalar2=1e-5,
                                                 op0=ALU.mult, op1=ALU.add), inc="vev")
                O("V", lambda v: v.scalar_tensor_tensor(
                    out=stat_var[:, :], in0=stat_mean[:, :], scalar=-1.0,
                    in1=stat_mean[:, :], op0=ALU.mult, op1=ALU.mult), inc="vev", drain=True)
                O("V", lambda v: v.tensor_add(stat_var[:, :], stat_msq[:, :],
                                              stat_var[:, :]), inc="vev", drain=True)
                O("V", lambda v: v.reciprocal(stat_ln[sp][:, :], stat_var[:, :]),
                  waits=[("rstd", gr - 2)] if gr >= 2 else [],
                  inc="vev", tag=("var", gr), drain=True)
                O("V", lambda v, sp=sp: v.tensor_scalar(
                    out=stat_nmean[sp][:, :], in0=stat_mean[:, :], scalar1=-1.0,
                    scalar2=None, op0=ALU.mult), inc="vev", tag=("nmean", gr))

            def emit_stats_sn(gr):
                sp = gr % 2
                O("S", lambda s, sp=sp: s.activation(stat_rstd[sp][:, :], stat_ln[sp][:, :],
                                                     AF.Sqrt),
                  waits=[("var", gr)], inc="sev", tag=("rstd", gr))
                O("V", lambda v, sp=sp: v.tensor_mul(
                    stat_nmr[sp][:, :], stat_nmean[sp][:, :], stat_rstd[sp][:, :]),
                  waits=[("rstd", gr), ("nmean", gr)], inc="vev", tag=("nmr", gr), drain=True)

            emit_ldx(0)
            emit_stats(0)
            emit_stats_sn(0)
            for gr in range(NGR):
                xb = gr % 2
                sp = gr % 2
                for tt in range(4):
                    xnw = [("nmr", gr)]
                    if gr >= 2:
                        xnw.append(("tr", gr - 2, tt, 7, "xn"))
                    O("V", lambda v, xb=xb, tt=tt, sp=sp: v.tensor_scalar(
                        out=xn_stage[xb][:, tt, :], in0=x_stage[xb][:, tt, :],
                        scalar1=stat_rstd[sp][:, tt:tt + 1], scalar2=stat_nmr[sp][:, tt:tt + 1],
                        op0=ALU.mult, op1=ALU.add),
                      waits=xnw, inc="vev", tag=("xn", gr, tt), drain=(tt == 0))

                    for hc in range(HC):
                        if gr < OWN_GR:
                            dst = (lambda gr=gr, tt=tt, hc=hc:
                                   xn_own[:, hc, gr * 512 + tt * 128: gr * 512 + (tt + 1) * 128])
                        else:
                            dst = (lambda xb=xb, tt=tt, hc=hc:
                                   xnf_stage[xb][:, hc, tt * 128:(tt + 1) * 128])
                        xtra = [("xn", gr, tt)]
                        if gr >= 2 and gr - 2 >= OWN_GR:
                            xtra.append(("mmf", gr - 2, 7))
                        transpose_to(
                            (lambda xb=xb, tt=tt, hc=hc:
                             xn_stage[xb][:, tt, hc * 128:(hc + 1) * 128]),
                            dst, "S", xtra, (gr, tt, hc, "xn"))

                if gr + 1 < NGR:
                    emit_ldx(gr + 1)
                    emit_stats(gr + 1)
                for hoc in range(HC):
                    fb = hoc % 2
                    for hc in range(HC):
                        if gr < OWN_GR:
                            rhs = (lambda gr=gr, hc=hc: xn_own[:, hc, gr * 512:(gr + 1) * 512])
                        else:
                            rhs = (lambda xb=xb, hc=hc: xnf_stage[xb][:, hc, :])
                        ws = [("trev", gr, 3, hc, "xn")]
                        if hoc >= 2 and hc == 0:
                            ws.append(("evf", gr, hoc - 2))
                        if hc == 0:
                            ws.append("ld_all")
                        O("T", lambda t, hoc=hoc, fb=fb, hc=hc, rhs=rhs: t.matmul(
                            ps_f[fb][:, :], wt_sb[:, hc, hoc * 128:(hoc + 1) * 128], rhs(),
                            start=(hc == 0), stop=(hc == 7)),
                          waits=ws, inc=("tmm" if hc == 7 else None),
                          tag=(("mmf", gr, hoc) if hc == 7 else None))
                    ws = [("mmf", gr, hoc), "ld_all"]
                    if gr >= 2 and hoc == 0:
                        ws += [("fsfree", gr - 2), ("stf", gr - 2)]
                        if gr - 2 < OWN_GR:
                            ws.append(("fsq", gr - 2, 7))
                    O("S", lambda s, xb=xb, hoc=hoc, fb=fb: s.activation(
                        f_stage[xb][:, hoc, :], ps_f[fb][:, :], AF.Identity,
                        bias=battn_sb[:, hoc:hoc + 1], scale=1.0),
                      waits=ws, inc="sev", tag=("evf", gr, hoc))

                stfw = [("evf", gr, 7)]
                if gr >= 2:
                    stfw.append(("stf", gr - 2))
                O("Y", lambda y, gr=gr, xb=xb: y.dma_start(
                    ffm_d[:, :, gr * 512:(gr + 1) * 512], f_stage[xb][:, :, :]),
                  waits=stfw, inc="dff", tag=("stf", gr))

                for j4 in range(4):
                    jb = j4 % 2
                    jsem = "dta" if jb == 0 else "dtb"
                    for hoc in range(HC):
                        xtra = [("evf", gr, hoc)]
                        if hoc == 0:
                            k = gr * 2 + (j4 // 2)
                            if k >= 1:
                                xtra.append((jsem, 16 * k))
                        transpose_to(
                            (lambda xb=xb, hoc=hoc, j4=j4:
                             f_stage[xb][:, hoc, j4 * 128:(j4 + 1) * 128]),
                            (lambda jb=jb, hoc=hoc: ftm_stage[jb][:, hoc * 128:(hoc + 1) * 128]),
                            "V", xtra, (gr, j4, hoc, "ftm"))
                    O("Y", lambda y, gr=gr, j4=j4, jb=jb: y.dma_start(
                        ftm_d[:, gr * 4 + j4, :], ftm_stage[jb][:, :]),
                      waits=[("trev", gr, j4, 7, "ftm")], inc=jsem, tag=("stftm", gr, j4))
                w.ev[("fsfree", gr)] = w.ev[("tr", gr, 3, 7, "ftm")]
                if gr + 1 < NGR:
                    emit_stats_sn(gr + 1)

                if gr < OWN_GR:
                    for hoc in range(HC):
                        fq = hoc % 2
                        ws = [("evf", gr, hoc)]
                        if hoc >= 2:
                            ws.append(("mmq", gr, hoc - 2))
                        O("V", lambda v, xb=xb, hoc=hoc, fq=fq: v.tensor_mul(
                            fsq[fq][:, :], f_stage[xb][:, hoc, :], f_stage[xb][:, hoc, :]),
                          waits=ws, inc="vev", tag=("fsq", gr, hoc))
                        O("T", lambda t, fq=fq, hoc=hoc: t.matmul(
                            ps_nsq[0:1, :], ones_col[:, :], fsq[fq][:, :],
                            start=(hoc == 0), stop=(hoc == 7)),
                          waits=[("fsq", gr, hoc), "c_oc"] +
                                ([("evq", gr - 1)] if (hoc == 0 and gr >= 1) else []),
                          inc="tmm", tag=("mmq", gr, hoc))
                    O("V", lambda v, gr=gr: v.tensor_scalar(
                        out=negCf[0:1, gr * 512:(gr + 1) * 512], in0=ps_nsq[0:1, :],
                        scalar1=-1.0, scalar2=BSC * MAXD, op0=ALU.mult, op1=ALU.add),
                      waits=[("mmq", gr, 7)], inc="vev")
                    O("V", lambda v, gr=gr: v.tensor_scalar(
                        out=negCn[0:1, gr * 512:(gr + 1) * 512], in0=ps_nsq[0:1, :],
                        scalar1=-1.0, scalar2=None, op0=ALU.mult),
                      inc="vev", tag=("evq", gr))

                    for tt in range(4):
                        for hc in range(HC):
                            ws = [("trev", gr, tt, hc, "xn"), "ld_all"]
                            if hc == 0:
                                ws.append("ev_br" if prev_log[0] is None
                                          else ("evlog",) + prev_log[0])
                            O("T", lambda t, gr=gr, tt=tt, hc=hc: t.matmul(
                                ps_gate[:, :32],
                                xn_own[:, hc, gr * 512 + tt * 128: gr * 512 + (tt + 1) * 128],
                                gw_sb[:, hc, :], start=(hc == 0), stop=(hc == 7)),
                              waits=ws, inc=("tmm" if hc == 7 else None),
                              tag=(("mmg", gr, tt) if hc == 7 else None))
                        O("V", lambda v, gr=gr, tt=tt: v.tensor_copy(
                            logits_sb[:, gr * 4 + tt, :], ps_gate[:, :32]),
                          waits=[("mmg", gr, tt)], inc="vev", tag=("evlog", gr, tt))
                        prev_log[0] = (gr, tt)

                if gr == 3:
                    emit_routing()
                if gr == 5:
                    for t16 in range(16):
                        transpose_to((lambda t16=t16: rt_comb[:, t16, :]),
                                     (lambda t16=t16: combT[:, t16 * 128:(t16 + 1) * 128]),
                                     "V", [("comb", 15)], (t16, "comb"), prows=32)
                    emit_crow_load(w, 0, [("trev", 15, "comb")])

            if debug_scratch:
                O("Y", lambda y: y.dma_start(comb_dbg[:, :], combT[:, :]),
                  waits=[("trev", 15, "comb")], inc="dmisc")
                O("Y", lambda y: y.dma_start(negc_dbg[0:1, :], negCf[:, :]),
                  waits=[("evq", 3)], inc="dmisc")
                O("Y", lambda y: y.dma_start(negc_dbg[1:2, :], negCn[:, :]),
                  inc="dmisc")


        def ph2(w):
            O = w.op
            bw = w.bar(bar1)
            lastA = [None, None, None, None]
            gfree = [None, None, None, None]

            O("G", lambda g: g.memset(crep_one[:, :], 1.0),
              waits=bw, inc="gco", tag="c_cone")

            def emit_loads(e):
                emit_moe_loads(w, e, [("mmD", (e - 2) * NTB + NTB - 1, 1)])
                if e < 32:
                    emit_crow_load(w, e, [("crepmm", e - 1, 3)])

            def crep_mms(e):
                par = e % 2
                csem = "dca" if par == 0 else "dcb"
                cval = w.ev[("ldc", e)][1]
                for ic in range(4):
                    O("T", lambda t, ic=ic, par=par: t.matmul(
                        psA[:, ic * 512:(ic + 1) * 512], ones_row[:, :],
                        crow[par][:, ic * 512:(ic + 1) * 512], start=True, stop=True),
                      waits=[(csem, cval), lastA[ic]],
                      inc="tmm", tag=("crepmm_t", e, ic))
                    O("V", lambda v, ic=ic, par=par: v.tensor_copy(
                        crep_sb[par][:, ic, :], psA[:, ic * 512:(ic + 1) * 512]),
                      waits=[("crepmm_t", e, ic)], inc="vev", tag=("crepmm", e, ic))
                    lastA[ic] = ("crepmm", e, ic)

            def down_half(v_, h):
                e_, tb_ = v_ // NTB, v_ % NTB
                par = e_ % 2
                wsem = "dwa" if par == 0 else "dwb"
                wval = w.ev[("ldw", e_)][1]
                for hcq in range(4):
                    hcol = h * 4 + hcq
                    for ic in range(4):
                        ws = []
                        if hcq == 0 and ic == 0:
                            ws = [("zmul", v_, 3), (wsem, wval)]
                            if h == 0 and v_ >= 1:
                                ws.append(("add", v_ - 1, 1))
                            if h == 1:
                                ws.append(("add", v_, 0))
                        last = hcq == 3 and ic == 3
                        O("T", lambda t, par=par, ic=ic, hcol=hcol, hcq=hcq, v_=v_: t.matmul(
                            psB[:, hcq * 512:(hcq + 1) * 512],
                            wd_sb2[par][:, ic, hcol * 128:(hcol + 1) * 128],
                            z_sb[v_ % 3][:, ic, :], start=(ic == 0), stop=(ic == 3)),
                          waits=ws, inc=("tmm" if last else None),
                          tag=(("mmD", v_, h) if last else None))
                e0 = e_ == 0

                def addfn(v, v_=v_, h=h, tb_=tb_, e0=e0):
                    dst = acc_sb[:, tb_, h, :, :].rearrange("p a b -> p (a b)")
                    if e0:
                        return v.tensor_copy(dst, psB[:, :])
                    return v.tensor_tensor(dst, dst, psB[:, :], ALU.add)
                O("V", addfn, waits=[("mmD", v_, h)], inc="vev", tag=("add", v_, h))

            crep_mms(0)
            emit_crow_load(w, 1, [("crepmm", 0, 3)])

            for u in range(NU):
                e, tb = u // NTB, u % NTB
                par = e % 2
                zb = u % 2
                wsem = "dwa" if par == 0 else "dwb"
                wval = w.ev[("ldw", e)][1]

                if u >= 2:
                    down_half(u - 2, 0)
                if tb == 2 and (e + 1) < 32:
                    crep_mms(e + 1)

                for ic in range(4):
                    for hc in range(HC):
                        ws = []
                        if hc == 0:
                            ws = [lastA[ic], (wsem, wval)]
                            if u == 0 and ic == 0:
                                ws += list(bw)
                        O("T", lambda t, par=par, ic=ic, hc=hc, tb=tb: t.matmul(
                            psA[:, ic * 512:(ic + 1) * 512],
                            wg_sb2[par][:, hc, ic * 128:(ic + 1) * 128],
                            xn_own[:, hc, tb * 512:(tb + 1) * 512],
                            start=(hc == 0), stop=(hc == 7)),
                          waits=ws, inc=("tmm" if hc == 7 else None),
                          tag=(("mmGi", u, ic) if hc == 7 else None))
                    ws = [("mmGi", u, ic)]
                    if u >= 2:
                        ws.append(("zmul", u - 2, ic))
                    O("S", lambda s, zb=zb, ic=ic: s.activation(
                        g_sb[zb][:, ic, :], psA[:, ic * 512:(ic + 1) * 512], AF.Silu),
                      waits=ws, inc="sev", tag=("silu", u, ic))

                if u >= 2:
                    down_half(u - 2, 1)
                if tb == 1 and u >= 5 and (e + 1) < NE:
                    emit_loads(e + 1)

                for ic in range(4):
                    for hc in range(HC):
                        ws = [("silu", u, ic)] if hc == 0 else []
                        O("T", lambda t, par=par, ic=ic, hc=hc, tb=tb: t.matmul(
                            psA[:, ic * 512:(ic + 1) * 512],
                            wu_sb2[par][:, hc, ic * 128:(ic + 1) * 128],
                            xn_own[:, hc, tb * 512:(tb + 1) * 512],
                            start=(hc == 0), stop=(hc == 7)),
                          waits=ws, inc=("tmm" if hc == 7 else None),
                          tag=(("mmUi", u, ic) if hc == 7 else None))
                    if e >= 32:
                        crep_fn = (lambda tb=tb: crep_one[:, tb * 512:(tb + 1) * 512])
                    else:
                        crep_fn = (lambda par=par, tb=tb: crep_sb[par][:, tb, :])
                    ws = [("mmUi", u, ic)]
                    if e >= 32 and ic == 0:
                        ws.append("c_cone")
                    if u >= 2:
                        ws.append(("zmul", u - 2, ic))
                    O("V", lambda v, zb=zb, ic=ic, crep_fn=crep_fn: v.scalar_tensor_tensor(
                        out=u_sb[zb][:, ic, :], in0=psA[:, ic * 512:(ic + 1) * 512],
                        scalar=1.0, in1=crep_fn(), op0=ALU.bypass, op1=ALU.mult),
                      waits=ws, inc="vev", tag=("stt", u, ic))
                    lastA[ic] = ("stt", u, ic)
                for ic in range(4):
                    zw = [("stt", u, ic)]
                    if u >= 3 and ic == 0:
                        zw.append(("mmD", u - 3, 1))
                    O("V", lambda v, u=u, zb=zb, ic=ic: v.tensor_mul(
                        z_sb[u % 3][:, ic, :], g_sb[zb][:, ic, :], u_sb[zb][:, ic, :]),
                      waits=zw, inc="vev", tag=("zmul", u, ic), drain=(ic == 0))

            down_half(NU - 2, 0)
            down_half(NU - 2, 1)
            down_half(NU - 1, 0)
            down_half(NU - 1, 1)

            gtr = [0]
            for tb in range(NTB):
                ws = [("add", 33 * NTB + tb, 0), ("add", 33 * NTB + tb, 1)]
                if tb >= 1:
                    ws.append(("gsilfree", tb - 1))
                O("S", lambda s, tb=tb: s.activation(
                    g_sil[:, :, :, :], acc_sb[:, tb, :, :, :], AF.Silu),
                  waits=ws, inc="sev", tag=("gsil", tb))
                for tt4 in range(4):
                    tpar = (tb * 4 + tt4) % 2
                    for hcol in range(HC):
                        half, hcq = hcol // 4, hcol % 4
                        k = gtr[0] % 4
                        ws = [("gsil", tb), gfree[k]]
                        if gtr[0] == 0:
                            ws.append(("add", NU - 1, 1))
                        if hcol == 0 and (tb * 4 + tt4) >= 2:
                            ws.append(("dgw", 16 * (tb * 4 + tt4 - 1)))
                        O("T", lambda t, tb=tb, half=half, hcq=hcq, tt4=tt4, k=k: t.transpose(
                            psB[:, k * 512:k * 512 + 128],
                            g_sil[:, half, hcq, tt4 * 128:(tt4 + 1) * 128], ident_f32[:, :]),
                          waits=ws, inc="tmm", tag=("gtr", gtr[0]))
                        O("V", lambda v, tpar=tpar, hcol=hcol, k=k: v.tensor_copy(
                            gtm_stage[tpar][:, hcol * 128:(hcol + 1) * 128],
                            psB[:, k * 512:k * 512 + 128]),
                          waits=[("gtr", gtr[0])], inc="vev", tag=("gtrev", gtr[0]))
                        gfree[k] = ("gtrev", gtr[0])
                        gtr[0] += 1
                    O("Y", lambda y, tb=tb, tt4=tt4, tpar=tpar: y.dma_start(
                        gscr_d[:, tb * 4 + tt4, :], gtm_stage[tpar][:, :]),
                      waits=[("gtrev", gtr[0] - 1)], inc="dgw", tag=("stg", tb * 4 + tt4))
                w.ev[("gsilfree", tb)] = w.ev[("gtr", gtr[0] - 1)]

        def ph3(w):
            O = w.op
            bw = w.bar(bar2)
            O("Y", lambda y: y.dma_start(f_fm[:, :, 0:256], ffm_d[:, :, 0:256]),
              waits=bw, inc="dffr", tag="ldffm0")
            O("Y", lambda y: y.dma_start(f_fm[:, :, 256:], ffm_d[:, :, 256:]),
              inc="dffr", tag="ldffm")

            def emit_band_load(st):
                sb_ = st % 2
                bsem = "dba" if sb_ == 0 else "dbb"
                bandw = list(bw) if st < 2 else [("bandlast", st - 2)]
                O("G", lambda g, st=st, sb_=sb_: g.dma_start(
                    bands_sb[sb_][:, :, :], bands_d[st, :, :, :]),
                  waits=bandw, inc=bsem, tag=("ldband", st))

            def emit_g_load(st):
                sb_ = st % 2
                gsem = "dca" if sb_ == 0 else "dcb"
                gw_ = list(bw) if st < 2 else [("fin", st - 2, 1)]
                O("Y", lambda y, st=st, sb_=sb_: y.dma_start(
                    gbuf[sb_][:, :, :], gscr_d[:, st * 2:st * 2 + 2, :]),
                  waits=gw_, inc=gsem, tag=("ldg", st))

            def emit_x_load(st):
                xw_ = list(bw) if st < 1 else [("fin", st - 1, 1)]
                O("Y", lambda y, st=st: y.dma_start(
                    xbuf3[:, :, :],
                    x_d[st * 256:(st + 1) * 256, :].rearrange("(a p) d -> p a d", p=128)),
                  waits=xw_, inc="dxr", tag=("ldx3", st))

            def emit_pv(st, jc):
                pb = jc % 2
                ch, cpar = jc // 4, (jc // 4) % 2
                csem = "dra" if cpar == 0 else "drb"
                pvw = [("exp", st, jc), (csem, w.ev[("ldftm", st, ch)][1])]
                if jc == 0 and st >= 1:
                    pvw.append(("fin", st - 1, 1))
                first, last = jc == 0, jc == NKB - 1
                for q in range(2):
                    for nh in range(2):
                        O("T", lambda t, q=q, nh=nh, pb=pb, cpar=cpar, jc=jc,
                                 first=first, last=last: t.matmul(
                            psAcc[q][:, nh * 512:(nh + 1) * 512],
                            pt_sb[pb][:, q * 128:(q + 1) * 128],
                            ftm_buf[cpar][:, jc % 4, nh * 512:(nh + 1) * 512],
                            start=first, stop=last, skip_group_check=True),
                          waits=pvw if (q == 0 and nh == 0) else [], inc=None)
                    O("T", lambda t, q=q, pb=pb, first=first, last=last: t.matmul(
                        psRS[q][:, 0:1], pt_sb[pb][:, q * 128:(q + 1) * 128],
                        ones_col[:, :], start=first, stop=last, skip_group_check=True),
                      waits=[], inc=("tmm" if q == 1 else None),
                      tag=(("pv", st, jc) if q == 1 else None))

            def emit_finale(st):
                sb_ = st % 2
                O("V", lambda v: v.tensor_copy(rs_sb[:, 0:1], psRS[0][:, 0:1]),
                  waits=[("pv", st, NKB - 1)], inc="vev")
                O("V", lambda v: v.tensor_copy(rs_sb[:, 1:2], psRS[1][:, 0:1]), inc="vev")
                O("V", lambda v: v.reciprocal(rec_sb[:, :], rs_sb[:, :]),
                  inc="vev", drain=True)
                for q in range(2):
                    ws = [("ldg", st)] if q == 0 else []
                    O("V", lambda v, q=q, sb_=sb_: v.scalar_tensor_tensor(
                        out=sg_sb[:, q, :], in0=psAcc[q][:, :], scalar=rec_sb[:, q:q + 1],
                        in1=gbuf[sb_][:, q, :], op0=ALU.mult, op1=ALU.mult),
                      waits=ws, inc="vev", drain=(q == 0))
                for q in range(2):
                    ws = [("dxr", w.ev[("ldx3", st)][1])] if q == 0 else []
                    O("V", lambda v, q=q: v.tensor_add(sg_sb[:, q, :], sg_sb[:, q, :],
                                                       xbuf3[:, q, :]),
                      waits=ws, inc="vev", tag=("fin", st, q))
                O("Y", lambda y, st=st: y.dma_start(
                    out_d[st * 256:(st + 1) * 256, :].rearrange("(a p) d -> p a d", p=128),
                    sg_sb[:, :, :]),
                  waits=[("fin", st, 1)], inc="dor", tag=("stout", st))

            emit_band_load(0)
            emit_band_load(1)
            emit_g_load(0)
            emit_g_load(1)
            emit_x_load(0)
            prev = None
            for st in range(NST):
                sb_ = st % 2
                bsem = "dba" if sb_ == 0 else "dbb"
                bkbs = BAND_KBS[st]
                for jc in range(NKB):
                    pb = jc % 2
                    is_band = jc in bkbs
                    negc = negCn if is_band else negCf
                    for hc in range(HC):
                        ws = []
                        if hc == 0:
                            ws = [("dffr", 16 if (st == 0 and jc < 2) else 32)]
                            if st > 0 or jc >= 2:
                                pst, pjc = (st, jc - 2) if jc >= 2 else (st - 1, NKB - 2 + jc)
                                ws.append(("exp", pst, pjc))
                        O("T", lambda t, pb=pb, hc=hc, jc=jc, st=st: t.matmul(
                            psS[pb][:, :256], f_fm[:, hc, jc * 128:(jc + 1) * 128],
                            f_fm[:, hc, st * 256:(st + 1) * 256],
                            start=(hc == 0), stop=False, skip_group_check=True),
                          waits=ws, inc=None)
                    O("T", lambda t, pb=pb, negc=negc, st=st: t.matmul(
                        psS[pb][:, :256], ones_row[:, :], negc[0:1, st * 256:(st + 1) * 256],
                        start=False, stop=True, skip_group_check=True),
                      waits=[], inc="tmm", tag=("st", st, jc))
                    expw = [("st", st, jc)]
                    if is_band:
                        i = bkbs.index(jc)
                        O("V", lambda v, pb=pb, sb_=sb_, i=i: v.tensor_tensor(
                            psS[pb][:, :256], psS[pb][:, :256], bands_sb[sb_][:, i, :],
                            ALU.add),
                          waits=[("st", st, jc), (bsem, w.ev[("ldband", st)][1])],
                          inc="vev", tag=("bandadd", st, i))
                        expw.append(("bandadd", st, i))
                        w.ev[("bandlast", st)] = w.ev[("bandadd", st, i)]
                    if st > 0 or jc >= 2:
                        pst, pjc = (st, jc - 2) if jc >= 2 else (st - 1, NKB - 2 + jc)
                        expw.append(("pv", pst, pjc))
                    O("S", lambda s, pb=pb: s.activation(pt_sb[pb][:, :], psS[pb][:, :256],
                                                         AF.Exp),
                      waits=expw, inc="sev", tag=("exp", st, jc))
                    if prev is not None:
                        emit_pv(*prev)
                    prev = (st, jc)
                    if jc % 4 == 0:
                        ch = jc // 4
                        cpar = ch % 2
                        csem = "dra" if cpar == 0 else "drb"
                        gch = st * 16 + ch
                        if gch < 2:
                            cw = list(bw)
                        else:
                            pst, pc = (gch - 2) // 16, (gch - 2) % 16
                            cw = [("pv", pst, pc * 4 + 3)]
                        O("Y", lambda y, st=st, ch=ch, cpar=cpar: y.dma_start(
                            ftm_buf[cpar][:, :, :], ftm_d[:, ch * 4:(ch + 1) * 4, :]),
                          waits=cw, inc=csem, tag=("ldftm", st, ch))
                    if jc == 0 and st >= 1:
                        emit_finale(st - 1)
                        emit_x_load(st)
                        if st + 1 < NST:
                            emit_band_load(st + 1)
                            emit_g_load(st + 1)
            emit_pv(NST - 1, NKB - 1)
            emit_finale(NST - 1)

            O("Y", None, waits=[("dor", w.c["dor"])])

        ph12s = top.enter_context(ExitStack())
        xn_own = ph12s.enter_context(nc.sbuf_tensor("xn_own", [128, HC, TOK], BF16))
        combT = ph12s.enter_context(nc.sbuf_tensor("combT", [32, TOK], BF16))
        wg_sb2 = [ph12s.enter_context(nc.sbuf_tensor(f"wg_sb2{i}", [128, HC, NI], BF16))
                  for i in range(2)]
        wu_sb2 = [ph12s.enter_context(nc.sbuf_tensor(f"wu_sb2{i}", [128, HC, NI], BF16))
                  for i in range(2)]
        wd_sb2 = [ph12s.enter_context(nc.sbuf_tensor(f"wd_sb2{i}", [128, 4, D], BF16))
                  for i in range(2)]
        crow = [ph12s.enter_context(nc.sbuf_tensor("crow0", [1, TOK], BF16))]
        crow.append(crow[0])
        with ExitStack() as ph1s:
            P = ph1s.enter_context
            gb_bf = P(nc.sbuf_tensor("gb_bf", [1, 32], BF16))
            wt_sb = P(nc.sbuf_tensor("wt_sb", [128, HC, D], BF16))
            x_stage = [P(nc.sbuf_tensor(f"x_stage{i}", [128, 4, D], F32)) for i in range(2)]
            xn_stage = [P(nc.sbuf_tensor(f"xn_stage{i}", [128, 4, D], BF16)) for i in range(2)]
            xnf_stage = [P(nc.sbuf_tensor(f"xnf_stage{i}", [128, HC, 512], BF16))
                         for i in range(2)]
            f_stage = [P(nc.sbuf_tensor(f"f_stage{i}", [128, HC, 512], BF16)) for i in range(2)]
            ftm_stage = [P(nc.sbuf_tensor(f"ftm_stage{i}", [128, D], BF16)) for i in range(2)]
            fsq = [P(nc.sbuf_tensor(f"fsq{i}", [128, 512], BF16)) for i in range(2)]
            stat_sum = P(nc.sbuf_tensor("stat_sum", [128, 4], F32))
            stat_sumsq = P(nc.sbuf_tensor("stat_sumsq", [128, 4], F32))
            stat_mean = P(nc.sbuf_tensor("stat_mean", [128, 4], F32))
            stat_var = P(nc.sbuf_tensor("stat_var", [128, 4], F32))
            stat_msq = P(nc.sbuf_tensor("stat_msq", [128, 4], F32))
            stat_ln = [P(nc.sbuf_tensor(f"stat_ln{i}", [128, 4], F32)) for i in range(2)]
            stat_nmean = [P(nc.sbuf_tensor(f"stat_nmean{i}", [128, 4], F32)) for i in range(2)]
            stat_rstd = [P(nc.sbuf_tensor(f"stat_rstd{i}", [128, 4], F32)) for i in range(2)]
            stat_nmr = [P(nc.sbuf_tensor(f"stat_nmr{i}", [128, 4], F32)) for i in range(2)]
            logits_sb = P(nc.sbuf_tensor("logits_sb", [128, 16, 32], F32))
            scores_sb = P(nc.sbuf_tensor("scores_sb", [128, 16, 32], F32))
            biasrep = P(nc.sbuf_tensor("biasrep", [128, 32], F32))
            rt_sc = P(nc.sbuf_tensor("rt_sc", [128, 32], F32))
            rt_pair = P(nc.sbuf_tensor("rt_pair", [128, 48], F32))
            rt_m1 = P(nc.sbuf_tensor("rt_m1", [128, 8], F32))
            rt_m2 = P(nc.sbuf_tensor("rt_m2", [128, 8], F32))
            rt_m3 = P(nc.sbuf_tensor("rt_m3", [128, 8], F32))
            rt_gs = P(nc.sbuf_tensor("rt_gs", [128, 8], F32))
            rt_gs8 = P(nc.sbuf_tensor("rt_gs8", [128, 8], F32))
            rt_gmask = P(nc.sbuf_tensor("rt_gmask", [128, 8], F32))
            rt_smask = P(nc.sbuf_tensor("rt_smask", [128, 32], F32))
            rt_masked = P(nc.sbuf_tensor("rt_masked", [128, 32], F32))
            rt_tk8 = P(nc.sbuf_tensor("rt_tk8", [128, 8], F32))
            rt_emask = P(nc.sbuf_tensor("rt_emask", [128, 32], F32))
            rt_w8 = P(nc.sbuf_tensor("rt_w8", [128, 32], F32))
            rt_wsum = P(nc.sbuf_tensor("rt_wsum", [128, 1], F32))
            rt_rec = P(nc.sbuf_tensor("rt_rec", [128, 1], F32))
            rt_comb = P(nc.sbuf_tensor("rt_comb", [128, 16, 32], BF16))
            ps_f = [P(nc.psum_tensor(f"ps_f{i}", [128, 512], F32)) for i in range(2)]
            ps_tr = [P(nc.psum_tensor(f"ps_tr{i}", [128, 1024], BF16)) for i in range(4)]
            ps_nsq = P(nc.psum_tensor("ps_nsq", [128, 512], F32))
            ps_gate = P(nc.psum_tensor("ps_gate", [128, 512], F32))

            def ph1_run(w):
                ph1(w)
                bar1.update(w.snap())
                evc1.update(w.ev)

            with nc.Block() as blk:
                walks(blk, ph1_run)

        if phases >= 2:
            with ExitStack() as ph2s:
                P = ph2s.enter_context
                crep_sb = [P(nc.sbuf_tensor(f"crep_sb{i}", [128, 4, 512], BF16))
                           for i in range(2)]
                crep_one = P(nc.sbuf_tensor("crep_one", [128, TOK], BF16))
                g_sb = [P(nc.sbuf_tensor(f"g_sb{i}", [128, 4, 512], BF16)) for i in range(2)]
                u_sb = [P(nc.sbuf_tensor(f"u_sb{i}", [128, 4, 512], BF16)) for i in range(2)]
                z_sb = [P(nc.sbuf_tensor(f"z_sb{i}", [128, 4, 512], BF16)) for i in range(3)]
                acc_sb = P(nc.sbuf_tensor("acc_sb", [128, NTB, 2, 4, 512], BF16))
                g_sil = P(nc.sbuf_tensor("g_sil", [128, 2, 4, 512], F32))
                gtm_stage = [P(nc.sbuf_tensor(f"gtm_stage{i}", [128, D], BF16))
                             for i in range(2)]
                psA = P(nc.psum_tensor("psA", [128, 2048], F32))
                psB = P(nc.psum_tensor("psB", [128, 2048], F32))

                def ph2_run(w):
                    ph2(w)
                    bar2.update(w.snap())

                with nc.Block() as blk:
                    walks(blk, ph2_run, init=bar1, init_ev=evc1)

        ph12s.close()
        if phases >= 3:
            with ExitStack() as ph3s:
                P = ph3s.enter_context
                f_fm = P(nc.sbuf_tensor("f_fm", [128, HC, S], BF16))
                ftm_buf = [P(nc.sbuf_tensor(f"ftm_buf{i}", [128, 4, D], BF16))
                           for i in range(2)]
                pt_sb = [P(nc.sbuf_tensor(f"pt_sb{i}", [128, 256], BF16)) for i in range(2)]
                bands_sb = [P(nc.sbuf_tensor(f"bands_sb{i}", [128, 4, 256], BF16))
                            for i in range(2)]
                gbuf = [P(nc.sbuf_tensor(f"gbuf{i}", [128, 2, D], BF16)) for i in range(2)]
                xbuf3 = P(nc.sbuf_tensor("xbuf3", [128, 2, D], F32))
                sg_sb = P(nc.sbuf_tensor("sg_sb", [128, 2, D], F32))
                rs_sb = P(nc.sbuf_tensor("rs_sb", [128, 2], F32))
                rec_sb = P(nc.sbuf_tensor("rec_sb", [128, 2], F32))
                rsdbg_sb = P(nc.sbuf_tensor("rsdbg_sb", [128, 4], F32))
                psS = [P(nc.psum_tensor(f"psS{i}", [128, 512], F32)) for i in range(2)]
                psAcc = [P(nc.psum_tensor(f"psAcc{i}", [128, 1024], F32)) for i in range(2)]
                psRS = [P(nc.psum_tensor(f"psRS{i}", [128, 512], F32)) for i in range(2)]

                with nc.Block() as blk:
                    walks(blk, ph3, init=bar2)

    return nc



def _build_bands(off):
    bands = np.zeros((NST, 128, 4, 256), np.float32)
    for st in range(NST):
        for i, kb in enumerate(BAND_KBS[st]):
            kk = np.arange(128)
            qq = np.arange(256)
            j_true = (kb * 128 + kk + off) % S
            q_true = st * 256 + qq + off
            d = np.abs(q_true[None, :] - j_true[:, None]).astype(np.float32)
            bands[st, :, i, :] = BSC * np.minimum(d, float(MAXD))
    return bands


def prep_inputs(L0_attn_x, L0_attn_W, L0_attn_b, L0_ffn_gate_w, L0_ffn_gate_bias,
                L0_ffn_w_gate, L0_ffn_w_up, L0_ffn_w_down, L0_ffn_sh_gate,
                L0_ffn_sh_up, L0_ffn_sh_down):
    x = np.asarray(L0_attn_x, np.float32)
    wt = np.ascontiguousarray(np.asarray(L0_attn_W, np.float32).T)
    battn = np.ascontiguousarray(np.asarray(L0_attn_b, np.float32).reshape(D, 1))
    gw = np.ascontiguousarray(np.asarray(L0_ffn_gate_w, np.float32).T)
    gb = np.ascontiguousarray(np.asarray(L0_ffn_gate_bias, np.float32).reshape(1, 32))

    wg = np.asarray(L0_ffn_w_gate, np.float32)
    wu = np.asarray(L0_ffn_w_up, np.float32)
    wd = np.asarray(L0_ffn_w_down, np.float32)
    shg = np.asarray(L0_ffn_sh_gate, np.float32)
    shu = np.asarray(L0_ffn_sh_up, np.float32)
    shd = np.asarray(L0_ffn_sh_down, np.float32)

    moe = np.empty((NE, 3, NI * D), np.float32)
    for e in range(32):
        moe[e, 0] = wg[e].T.reshape(-1)
        moe[e, 1] = wu[e].T.reshape(-1)
        moe[e, 2] = wd[e].T.reshape(-1)
    shdT = np.ascontiguousarray(shd.T)
    for k in range(2):
        e = 32 + k
        moe[e, 0] = shg[k * NI:(k + 1) * NI, :].T.reshape(-1)
        moe[e, 1] = shu[k * NI:(k + 1) * NI, :].T.reshape(-1)
        moe[e, 2] = shdT[k * NI:(k + 1) * NI, :].reshape(-1)

    bands_cache = {}
    maps = []
    for c in range(8):
        b, off = c // 4, (c % 4) * TOK
        if off not in bands_cache:
            bands_cache[off] = _build_bands(off)
        xr = np.ascontiguousarray(np.roll(x[b], -off, axis=0))
        maps.append(dict(x=xr, wt=wt, battn=battn, gw=gw, gb=gb, moe=moe,
                         bands=bands_cache[off]))
    return maps


def kernel(**inputs):
    in_maps = prep_inputs(**inputs)
    nc = build_program()
    res = run_bass_kernel_spmd(nc, in_maps, core_ids=list(range(8)))
    out = np.empty((B, S, D), np.float32)
    for c in range(8):
        b, off = c // 4, (c % 4) * TOK
        out[b, off:off + TOK, :] = res.results[c]["out"]
    return out


# version 07
# speedup vs baseline: 1.2687x; 1.0047x over previous
import numpy as np
from contextlib import ExitStack

import concourse.bass as bass
import concourse.mybir as mybir
from concourse.bass_utils import run_bass_kernel_spmd

F32 = mybir.dt.float32
BF16 = mybir.dt.bfloat16
AF = mybir.ActivationFunctionType
ALU = mybir.AluOpType
AX = mybir.AxisListType

B, S, D = 2, 8192, 1024
HC = 8
TOK = 2048
NGR = 16
OWN_GR = 4
NE = 34
NI = 512
NTB = 4
NU = NE * NTB
NST = 8
NKB = 64
MAXD = 10
BSC = 0.01

ENGINE_OWN_SEM = {"T": "tmm", "V": "vev", "S": "sev", "G": "gco"}
DMA_SEMS = ("dxa", "dxb", "dff", "dffr", "dta", "dtb", "dra", "drb",
            "dwa", "dwb", "dca", "dcb", "dba", "dbb", "dgr", "dxr",
            "dor", "dmisc", "dgw")
CORE_SEMS = ("tmm", "vev", "sev", "gco")
ALL_SEMS = CORE_SEMS + DMA_SEMS


class Walk:

    def __init__(self, key, eng, sems):
        self.key = key
        self.eng = eng
        self.sems = sems
        self.c = {s: 0 for s in ALL_SEMS}
        self.ev = {}
        self.waited = {s: 0 for s in ALL_SEMS}

    def snap(self):
        return dict(self.c)

    def bar(self, snap):
        return [(s, v) for s, v in snap.items() if v > 0]

    def op(self, key, fn, waits=(), inc=None, inc_amt=None, tag=None, drain=False):
        post = None
        if inc is not None:
            amt = inc_amt if inc_amt is not None else (16 if inc in DMA_SEMS else 1)
            self.c[inc] += amt
            post = self.c[inc]
            if tag is not None:
                self.ev[tag] = (inc, post)
        if self.key == key:
            own = ENGINE_OWN_SEM.get(self.key)
            for wv in waits:
                if wv is None:
                    continue
                if isinstance(wv, tuple) and isinstance(wv[0], str) and wv[0] in ALL_SEMS:
                    s, v = wv
                else:
                    s, v = self.ev[wv]
                if s == own:
                    continue
                if v <= self.waited[s]:
                    continue
                self.waited[s] = v
                self.eng.wait_ge(self.sems[s], v)
            if fn is not None:
                if drain:
                    self.eng.drain()
                ins = fn(self.eng)
                if inc is not None:
                    ins.then_inc(self.sems[inc], amt)
        return post


def _band_instances():
    out = []
    for st in range(NST):
        insts = []
        for rel in (128, 0, -128, -256):
            kb = (st * 256 - rel) // 128
            if kb < 0:
                kb = NKB - 1
            insts.append(kb)
        out.append(insts)
    return out


BAND_KBS = _band_instances()


def build_program(phases=3, debug_scratch=False):
    nc = bass.Bass("TRN2", target_bir_lowering=False)

    x_d = nc.dram_tensor("x", [S, D], F32, kind="ExternalInput")
    wt_d = nc.dram_tensor("wt", [D, D], F32, kind="ExternalInput")
    battn_d = nc.dram_tensor("battn", [D, 1], F32, kind="ExternalInput")
    gw_d = nc.dram_tensor("gw", [D, 32], F32, kind="ExternalInput")
    gb_d = nc.dram_tensor("gb", [1, 32], F32, kind="ExternalInput")
    moe_d = nc.dram_tensor("moe", [NE, 3, NI * D], F32, kind="ExternalInput")
    bands_d = nc.dram_tensor("bands", [NST, 128, 4, 256], F32, kind="ExternalInput")
    out_d = nc.dram_tensor("out", [TOK, D], F32, kind="ExternalOutput")

    skind = "ExternalOutput" if debug_scratch else "Internal"
    ffm_d = nc.dram_tensor("ffm", [128, HC, S], BF16, kind=skind)
    ftm_d = nc.dram_tensor("ftm", [128, NKB, D], BF16, kind=skind)
    gscr_d = nc.dram_tensor("gscr", [128, 16, D], BF16, kind=skind)
    comb_dbg = None
    if debug_scratch:
        comb_dbg = nc.dram_tensor("combdbg", [32, TOK], BF16, kind="ExternalOutput")
        negc_dbg = nc.dram_tensor("negcdbg", [2, TOK], BF16, kind="ExternalOutput")
        rs_dbg = nc.dram_tensor("rsdbg", [NST, 128, 4], F32, kind="ExternalOutput")

    with ExitStack() as top:
        E = top.enter_context
        sems = {s: E(nc.semaphore(s)) for s in ALL_SEMS}

        ones_row = E(nc.sbuf_tensor("ones_row", [1, 128], BF16))
        ones_col = E(nc.sbuf_tensor("ones_col", [128, 1], BF16))
        ident_bf = E(nc.sbuf_tensor("ident_bf", [128, 128], BF16))
        ident_f32 = E(nc.sbuf_tensor("ident_f32", [128, 128], F32))
        negCf = E(nc.sbuf_tensor("negCf", [1, TOK], BF16))
        negCn = E(nc.sbuf_tensor("negCn", [1, TOK], BF16))
        battn_sb = E(nc.sbuf_tensor("battn_sb", [128, HC], F32))
        gw_sb = E(nc.sbuf_tensor("gw_sb", [128, HC, 32], BF16))

        def walks(block, fn, init=None, init_ev=None):
            def run(key):
                def _f(eng):
                    w = Walk(key, eng, sems)
                    if init:
                        w.c.update(init)
                    if init_ev:
                        w.ev.update(init_ev)
                    fn(w)
                return _f
            block.tensor(run("T"))
            block.vector(run("V"))
            block.scalar(run("S"))
            block.sync(run("Y"))
            block.gpsimd(run("G"))

        bar1 = {}
        bar2 = {}
        evc1 = {}

        def emit_moe_loads(w, e, first_waits):
            O = w.op
            par = e % 2
            wsem = "dwa" if par == 0 else "dwb"
            srcs = [
                lambda e=e: moe_d[e, 0, :].rearrange("(hc p i) -> p hc i", p=128, i=NI),
                lambda e=e: moe_d[e, 1, :].rearrange("(hc p i) -> p hc i", p=128, i=NI),
                lambda e=e: moe_d[e, 2, :].rearrange("(ic p h) -> p ic h", p=128, h=D),
            ]
            for m, dst in ((0, wg_sb2), (1, wu_sb2), (2, wd_sb2)):
                O("G", lambda g, dst=dst, par=par, src=srcs[m]: g.dma_start(
                    dst[par][:, :, :], src()),
                  waits=first_waits if m == 0 else (), inc=wsem,
                  tag=(("ldw", e) if m == 2 else None))

        def emit_crow_load(w, e, waits):
            O = w.op
            par = e % 2
            csem = "dca" if par == 0 else "dcb"
            O("Y", lambda y, e=e, par=par: y.dma_start(crow[par][:, :], combT[e:e + 1, :]),
              waits=waits, inc=csem, tag=("ldc", e))

        def ph1(w):
            O = w.op
            emit_moe_loads(w, 0, [])
            emit_moe_loads(w, 1, [])
            O("G", lambda g: g.memset(ones_row[:, :], 1.0), inc="gco", tag="c_or")
            O("G", lambda g: g.memset(ones_col[:, :], 1.0), inc="gco", tag="c_oc")
            O("G", lambda g: g.iota(x_stage[0][:, 0, 0:128], [[1, 128]], channel_multiplier=0,
                                    allow_small_or_imprecise_dtypes=True), inc="gco", tag="c_i1")
            O("G", lambda g: g.iota(x_stage[0][:, 1, 0:128], [[0, 128]], channel_multiplier=1,
                                    allow_small_or_imprecise_dtypes=True), inc="gco", tag="c_i2")
            O("V", lambda v: v.tensor_tensor(ident_bf[:, :], x_stage[0][:, 0, 0:128],
                                             x_stage[0][:, 1, 0:128], ALU.is_equal),
              waits=["c_i1", "c_i2"], inc="vev", tag="c_idb")
            O("V", lambda v: v.tensor_tensor(ident_f32[:, :], x_stage[0][:, 0, 0:128],
                                             x_stage[0][:, 1, 0:128], ALU.is_equal),
              inc="vev", tag="c_idf")
            O("G", lambda g: g.dma_start(wt_sb[:, :, :],
                                         wt_d[:, :].rearrange("(hc p) ho -> p hc ho", p=128)),
              inc="dmisc", tag="ld_wt")
            O("G", lambda g: g.dma_start(gw_sb[:, :, :],
                                         gw_d[:, :].rearrange("(hc p) e -> p hc e", p=128)),
              inc="dmisc", tag="ld_gw")
            O("G", lambda g: g.dma_start(gb_bf[:, :], gb_d[:, :]), inc="dmisc", tag="ld_gb")
            O("Y", lambda y: y.dma_start(battn_sb[:, :],
                                         battn_d[:, :].rearrange("(hc p) o -> p (hc o)", p=128),
                                         allow_slow_non_contiguous=True),
              inc="dmisc", tag="ld_ba")
            w.ev["ld_all"] = ("dmisc", w.c["dmisc"])
            O("T", lambda t: t.matmul(ps_gate[:, :32], ones_row[:, :], gb_bf[:, :],
                                      start=True, stop=True),
              waits=["ld_all", "c_or"], inc="tmm", tag="mm_br")
            O("V", lambda v: v.tensor_copy(biasrep[:, :], ps_gate[:, :32]),
              waits=["mm_br"], inc="vev", tag="ev_br")

            trcnt = [0]
            trfree = [None, None, None, None]

            def transpose_to(src_fn, dst_fn, dst_eng, extra_waits, tagbase, prows=128):
                b = trcnt[0] % 4
                O("T", lambda t, b=b, prows=prows: t.transpose(
                    ps_tr[b][0:prows, :128], src_fn(), ident_bf[:, :]),
                  waits=list(extra_waits) + ["c_idb", trfree[b]],
                  inc="tmm", tag=("tr",) + tagbase)
                if dst_eng == "S":
                    O("S", lambda s, b=b, prows=prows: s.activation(
                        dst_fn(), ps_tr[b][0:prows, :128], AF.Copy),
                      waits=[("tr",) + tagbase], inc="sev", tag=("trev",) + tagbase)
                else:
                    O("V", lambda v, b=b, prows=prows: v.tensor_copy(
                        dst_fn(), ps_tr[b][0:prows, :128]),
                      waits=[("tr",) + tagbase], inc="vev", tag=("trev",) + tagbase)
                trfree[b] = ("trev",) + tagbase
                trcnt[0] += 1

            def transpose_b4(src_fns, dst_fn, dst_eng, extra_waits, tagbase):
                b = trcnt[0] % 4
                for k, sf in enumerate(src_fns):
                    ws = (list(extra_waits) + ["c_idb", trfree[b]]) if k == 0 else []
                    O("T", lambda t, b=b, k=k, sf=sf: t.matmul(
                        ps_tr[b][:, k * 128:(k + 1) * 128], sf(), ident_bf[:, :],
                        is_transpose=True, start=(k == 0), stop=(k == 3),
                        skip_group_check=True),
                      waits=ws, inc=("tmm" if k == 3 else None),
                      tag=(("tr",) + tagbase if k == 3 else None))
                if dst_eng == "S":
                    O("S", lambda s, b=b: s.activation(dst_fn(), ps_tr[b][:, 0:512],
                                                       AF.Copy),
                      waits=[("tr",) + tagbase], inc="sev", tag=("trev",) + tagbase)
                else:
                    O("V", lambda v, b=b: v.tensor_copy(dst_fn(), ps_tr[b][:, 0:512]),
                      waits=[("tr",) + tagbase], inc="vev", tag=("trev",) + tagbase)
                trfree[b] = ("trev",) + tagbase
                trcnt[0] += 1

            prev_log = [None]

            def emit_routing():
                O = w.op
                O("S", lambda s: s.activation(scores_sb[:, :, :], logits_sb[:, :, :], AF.Sigmoid),
                  waits=[("evlog", 3, 3)], inc="sev", tag="s_scores")
                PAIRS = [(0, 1), (0, 2), (0, 3), (1, 2), (1, 3), (2, 3)]
                for t16 in range(16):
                    O("V", lambda v, t16=t16: v.tensor_add(rt_sc[:, :], scores_sb[:, t16, :],
                                                           biasrep[:, :]),
                      waits=["s_scores", "ev_br"], inc="vev")
                    for i, (a, b) in enumerate(PAIRS):
                        O("V", lambda v, i=i, a=a, b=b: v.tensor_add(
                            rt_pair[:, i * 8:(i + 1) * 8], rt_sc[:, a:32:4], rt_sc[:, b:32:4]),
                          inc="vev", drain=(i == 0))
                    O("V", lambda v: v.tensor_max(rt_m1[:, :], rt_pair[:, 0:8], rt_pair[:, 8:16]),
                      inc="vev")
                    O("V", lambda v: v.tensor_max(rt_m2[:, :], rt_pair[:, 16:24], rt_pair[:, 24:32]),
                      inc="vev")
                    O("V", lambda v: v.tensor_max(rt_m3[:, :], rt_pair[:, 32:40], rt_pair[:, 40:48]),
                      inc="vev")
                    O("V", lambda v: v.tensor_max(rt_m1[:, :], rt_m1[:, :], rt_m2[:, :]),
                      inc="vev", drain=True)
                    O("V", lambda v: v.tensor_max(rt_gs[:, :], rt_m1[:, :], rt_m3[:, :]),
                      inc="vev", drain=True)
                    O("V", lambda v: v.max(out=rt_gs8[:, :], in_=rt_gs[:, :]),
                      inc="vev", drain=True)
                    O("V", lambda v: v.tensor_scalar(out=rt_gmask[:, :], in0=rt_gs[:, :],
                                                     scalar1=rt_gs8[:, 3:4], scalar2=None,
                                                     op0=ALU.is_ge), inc="vev", drain=True)
                    O("V", lambda v: v.tensor_copy(
                        rt_smask[:, :].rearrange("p (g j) -> p g j", j=4),
                        rt_gmask[:, :, None].broadcast_to([128, 8, 4])), inc="vev", drain=True)
                    O("V", lambda v: v.tensor_mul(rt_masked[:, :], rt_sc[:, :], rt_smask[:, :]),
                      inc="vev", drain=True)
                    O("V", lambda v: v.max(out=rt_tk8[:, :], in_=rt_masked[:, :]),
                      inc="vev", drain=True)
                    O("V", lambda v: v.tensor_scalar(out=rt_emask[:, :], in0=rt_masked[:, :],
                                                     scalar1=rt_tk8[:, 7:8], scalar2=None,
                                                     op0=ALU.is_ge), inc="vev", drain=True)
                    O("V", lambda v, t16=t16: v.tensor_mul(rt_w8[:, :], scores_sb[:, t16, :],
                                                           rt_emask[:, :]), inc="vev", drain=True)
                    O("V", lambda v: v.reduce_sum(rt_wsum[:, :], rt_w8[:, :], axis=AX.X),
                      inc="vev", drain=True)
                    O("V", lambda v: v.tensor_scalar(out=rt_wsum[:, :], in0=rt_wsum[:, :],
                                                     scalar1=1e-20, scalar2=None, op0=ALU.add),
                      inc="vev", drain=True)
                    O("V", lambda v: v.reciprocal(rt_rec[:, :], rt_wsum[:, :]),
                      inc="vev", drain=True)
                    O("V", lambda v, t16=t16: v.tensor_scalar(
                        out=rt_comb[:, t16, :], in0=rt_w8[:, :],
                        scalar1=rt_rec[:, :], scalar2=2.5,
                        op0=ALU.mult, op1=ALU.mult),
                      inc="vev", tag=("comb", t16), drain=True)
            def emit_ldx(gr):
                xb = gr % 2
                xsem = "dxa" if xb == 0 else "dxb"
                xw = [("xn", gr - 2, 3)] if gr >= 2 else (["c_idf"] if gr == 0 else [])
                O("Y", lambda y, gr=gr, xb=xb: y.dma_start(
                    x_stage[xb][:, :, :],
                    x_d[gr * 512:(gr + 1) * 512, :].rearrange("(a p) d -> p a d", p=128)),
                  waits=xw, inc=xsem, tag=("ldx", gr))

            def emit_stats(gr):
                xb = gr % 2
                xsem = "dxa" if xb == 0 else "dxb"
                sp = gr % 2
                ldxw = [(xsem, w.ev[("ldx", gr)][1])]
                O("V", lambda v, xb=xb: v.reduce_sum(
                    stat_sum[:, :, None], x_stage[xb][:, :, :], axis=AX.X),
                  waits=ldxw, inc="vev")
                sqw = [("tr", gr - 2, 3, 1, "xnb")] if gr >= 2 else []
                for tt in range(4):
                    O("V", lambda v, xb=xb, tt=tt: v.tensor_mul(
                        xn_stage[xb][:, 3, :], x_stage[xb][:, tt, :], x_stage[xb][:, tt, :]),
                      waits=sqw if tt == 0 else [], inc="vev")
                    O("V", lambda v, tt=tt, xb=xb: v.reduce_sum(
                        stat_sumsq[:, tt:tt + 1], xn_stage[xb][:, 3, :], axis=AX.X),
                      inc="vev", drain=True)
                O("V", lambda v: v.tensor_scalar(out=stat_mean[:, :], in0=stat_sum[:, :],
                                                 scalar1=1.0 / D, scalar2=None, op0=ALU.mult),
                  inc="vev", drain=True)
                O("V", lambda v: v.tensor_scalar(out=stat_msq[:, :], in0=stat_sumsq[:, :],
                                                 scalar1=1.0 / D, scalar2=1e-5,
                                                 op0=ALU.mult, op1=ALU.add), inc="vev")
                O("V", lambda v: v.scalar_tensor_tensor(
                    out=stat_var[:, :], in0=stat_mean[:, :], scalar=-1.0,
                    in1=stat_mean[:, :], op0=ALU.mult, op1=ALU.mult), inc="vev", drain=True)
                O("V", lambda v: v.tensor_add(stat_var[:, :], stat_msq[:, :],
                                              stat_var[:, :]), inc="vev", drain=True)
                O("V", lambda v: v.reciprocal(stat_ln[sp][:, :], stat_var[:, :]),
                  waits=[("rstd", gr - 2)] if gr >= 2 else [],
                  inc="vev", tag=("var", gr), drain=True)
                O("V", lambda v, sp=sp: v.tensor_scalar(
                    out=stat_nmean[sp][:, :], in0=stat_mean[:, :], scalar1=-1.0,
                    scalar2=None, op0=ALU.mult), inc="vev", tag=("nmean", gr))

            def emit_stats_sn(gr):
                sp = gr % 2
                O("S", lambda s, sp=sp: s.activation(stat_rstd[sp][:, :], stat_ln[sp][:, :],
                                                     AF.Sqrt),
                  waits=[("var", gr)], inc="sev", tag=("rstd", gr))
                O("V", lambda v, sp=sp: v.tensor_mul(
                    stat_nmr[sp][:, :], stat_nmean[sp][:, :], stat_rstd[sp][:, :]),
                  waits=[("rstd", gr), ("nmean", gr)], inc="vev", tag=("nmr", gr), drain=True)

            emit_ldx(0)
            emit_stats(0)
            emit_stats_sn(0)
            for gr in range(NGR):
                xb = gr % 2
                sp = gr % 2
                for tt in range(4):
                    xnw = [("nmr", gr)]
                    if gr >= 2:
                        xnw.append(("tr", gr - 2, tt, 1, "xnb"))
                    O("V", lambda v, xb=xb, tt=tt, sp=sp: v.tensor_scalar(
                        out=xn_stage[xb][:, tt, :], in0=x_stage[xb][:, tt, :],
                        scalar1=stat_rstd[sp][:, tt:tt + 1], scalar2=stat_nmr[sp][:, tt:tt + 1],
                        op0=ALU.mult, op1=ALU.add),
                      waits=xnw, inc="vev", tag=("xn", gr, tt), drain=(tt == 0))

                    for B in range(2):
                        hc0 = B * 4
                        if gr < OWN_GR:
                            dst = (lambda gr=gr, tt=tt, hc0=hc0: xn_own[
                                :, hc0:hc0 + 4,
                                gr * 512 + tt * 128: gr * 512 + (tt + 1) * 128])
                        else:
                            dst = (lambda xb=xb, tt=tt, hc0=hc0:
                                   xnf_stage[xb][:, hc0:hc0 + 4, tt * 128:(tt + 1) * 128])
                        xtra = [("xn", gr, tt)]
                        if gr >= 2 and gr - 2 >= OWN_GR:
                            xtra.append(("mmf", gr - 2, 7))
                        srcs = [(lambda xb=xb, tt=tt, hc=hc0 + k:
                                 xn_stage[xb][:, tt, hc * 128:(hc + 1) * 128])
                                for k in range(4)]
                        transpose_b4(srcs, dst, "S" if B == 0 else "V",
                                     xtra, (gr, tt, B, "xnb"))

                if gr + 1 < NGR:
                    emit_ldx(gr + 1)
                    emit_stats(gr + 1)
                for hoc in range(HC):
                    fb = hoc % 2
                    for hc in range(HC):
                        if gr < OWN_GR:
                            rhs = (lambda gr=gr, hc=hc: xn_own[:, hc, gr * 512:(gr + 1) * 512])
                        else:
                            rhs = (lambda xb=xb, hc=hc: xnf_stage[xb][:, hc, :])
                        ws = [("trev", gr, 3, hc // 4, "xnb")]
                        if hoc >= 2 and hc == 0:
                            ws.append(("evf", gr, hoc - 2))
                        if hc == 0:
                            ws.append("ld_all")
                        O("T", lambda t, hoc=hoc, fb=fb, hc=hc, rhs=rhs: t.matmul(
                            ps_f[fb][:, :], wt_sb[:, hc, hoc * 128:(hoc + 1) * 128], rhs(),
                            start=(hc == 0), stop=(hc == 7)),
                          waits=ws, inc=("tmm" if hc == 7 else None),
                          tag=(("mmf", gr, hoc) if hc == 7 else None))
                    ws = [("mmf", gr, hoc), "ld_all"]
                    if gr >= 2 and hoc == 0:
                        ws += [("fsfree", gr - 2), ("stf", gr - 2)]
                        if gr - 2 < OWN_GR:
                            ws.append(("fsq", gr - 2, 7))
                    O("S", lambda s, xb=xb, hoc=hoc, fb=fb: s.activation(
                        f_stage[xb][:, hoc, :], ps_f[fb][:, :], AF.Identity,
                        bias=battn_sb[:, hoc:hoc + 1], scale=1.0),
                      waits=ws, inc="sev", tag=("evf", gr, hoc))

                stfw = [("evf", gr, 7)]
                if gr >= 2:
                    stfw.append(("stf", gr - 2))
                O("Y", lambda y, gr=gr, xb=xb: y.dma_start(
                    ffm_d[:, :, gr * 512:(gr + 1) * 512], f_stage[xb][:, :, :]),
                  waits=stfw, inc="dff", tag=("stf", gr))

                for j4 in range(4):
                    jb = j4 % 2
                    jsem = "dta" if jb == 0 else "dtb"
                    for B in range(2):
                        hc0 = B * 4
                        xtra = [("evf", gr, hc0 + 3)]
                        if B == 0:
                            k = gr * 2 + (j4 // 2)
                            if k >= 1:
                                xtra.append((jsem, 16 * k))
                        srcs = [(lambda xb=xb, hoc=hc0 + k, j4=j4:
                                 f_stage[xb][:, hoc, j4 * 128:(j4 + 1) * 128])
                                for k in range(4)]
                        transpose_b4(srcs,
                                     (lambda jb=jb, hc0=hc0:
                                      ftm_stage[jb][:, hc0 * 128:(hc0 + 4) * 128]),
                                     "V" if B == 0 else "S", xtra, (gr, j4, B, "ftmb"))
                    O("Y", lambda y, gr=gr, j4=j4, jb=jb: y.dma_start(
                        ftm_d[:, gr * 4 + j4, :], ftm_stage[jb][:, :]),
                      waits=[("trev", gr, j4, 1, "ftmb")], inc=jsem, tag=("stftm", gr, j4))
                w.ev[("fsfree", gr)] = w.ev[("tr", gr, 3, 1, "ftmb")]
                if gr + 1 < NGR:
                    emit_stats_sn(gr + 1)

                if gr < OWN_GR:
                    for hoc in range(HC):
                        fq = hoc % 2
                        ws = [("evf", gr, hoc)]
                        if hoc >= 2:
                            ws.append(("mmq", gr, hoc - 2))
                        O("V", lambda v, xb=xb, hoc=hoc, fq=fq: v.tensor_mul(
                            fsq[fq][:, :], f_stage[xb][:, hoc, :], f_stage[xb][:, hoc, :]),
                          waits=ws, inc="vev", tag=("fsq", gr, hoc))
                        O("T", lambda t, fq=fq, hoc=hoc: t.matmul(
                            ps_nsq[0:1, :], ones_col[:, :], fsq[fq][:, :],
                            start=(hoc == 0), stop=(hoc == 7)),
                          waits=[("fsq", gr, hoc), "c_oc"] +
                                ([("evq", gr - 1)] if (hoc == 0 and gr >= 1) else []),
                          inc="tmm", tag=("mmq", gr, hoc))
                    O("V", lambda v, gr=gr: v.tensor_scalar(
                        out=negCf[0:1, gr * 512:(gr + 1) * 512], in0=ps_nsq[0:1, :],
                        scalar1=-1.0, scalar2=BSC * MAXD, op0=ALU.mult, op1=ALU.add),
                      waits=[("mmq", gr, 7)], inc="vev")
                    O("V", lambda v, gr=gr: v.tensor_scalar(
                        out=negCn[0:1, gr * 512:(gr + 1) * 512], in0=ps_nsq[0:1, :],
                        scalar1=-1.0, scalar2=None, op0=ALU.mult),
                      inc="vev", tag=("evq", gr))

                    for tt in range(4):
                        for hc in range(HC):
                            ws = [("trev", gr, tt, hc // 4, "xnb"), "ld_all"]
                            if hc == 0:
                                ws.append("ev_br" if prev_log[0] is None
                                          else ("evlog",) + prev_log[0])
                            O("T", lambda t, gr=gr, tt=tt, hc=hc: t.matmul(
                                ps_gate[:, :32],
                                xn_own[:, hc, gr * 512 + tt * 128: gr * 512 + (tt + 1) * 128],
                                gw_sb[:, hc, :], start=(hc == 0), stop=(hc == 7)),
                              waits=ws, inc=("tmm" if hc == 7 else None),
                              tag=(("mmg", gr, tt) if hc == 7 else None))
                        O("V", lambda v, gr=gr, tt=tt: v.tensor_copy(
                            logits_sb[:, gr * 4 + tt, :], ps_gate[:, :32]),
                          waits=[("mmg", gr, tt)], inc="vev", tag=("evlog", gr, tt))
                        prev_log[0] = (gr, tt)

                if gr == 3:
                    emit_routing()
                if gr == 5:
                    for t16 in range(16):
                        transpose_to((lambda t16=t16: rt_comb[:, t16, :]),
                                     (lambda t16=t16: combT[:, t16 * 128:(t16 + 1) * 128]),
                                     "V", [("comb", 15)], (t16, "comb"), prows=32)
                    emit_crow_load(w, 0, [("trev", 15, "comb")])

            if debug_scratch:
                O("Y", lambda y: y.dma_start(comb_dbg[:, :], combT[:, :]),
                  waits=[("trev", 15, "comb")], inc="dmisc")
                O("Y", lambda y: y.dma_start(negc_dbg[0:1, :], negCf[:, :]),
                  waits=[("evq", 3)], inc="dmisc")
                O("Y", lambda y: y.dma_start(negc_dbg[1:2, :], negCn[:, :]),
                  inc="dmisc")


        def ph2(w):
            O = w.op
            bw = w.bar(bar1)
            lastA = [None, None, None, None]
            gfree = [None, None, None, None]

            O("G", lambda g: g.memset(crep_one[:, :], 1.0),
              waits=bw, inc="gco", tag="c_cone")

            def emit_loads(e):
                emit_moe_loads(w, e, [("mmD", (e - 2) * NTB + NTB - 1, 1)])
                if e < 32:
                    emit_crow_load(w, e, [("crepmm", e - 1, 3)])

            def crep_mms(e):
                par = e % 2
                csem = "dca" if par == 0 else "dcb"
                cval = w.ev[("ldc", e)][1]
                for ic in range(4):
                    O("T", lambda t, ic=ic, par=par: t.matmul(
                        psA[:, ic * 512:(ic + 1) * 512], ones_row[:, :],
                        crow[par][:, ic * 512:(ic + 1) * 512], start=True, stop=True),
                      waits=[(csem, cval), lastA[ic]],
                      inc="tmm", tag=("crepmm_t", e, ic))
                    O("V", lambda v, ic=ic, par=par: v.tensor_copy(
                        crep_sb[par][:, ic, :], psA[:, ic * 512:(ic + 1) * 512]),
                      waits=[("crepmm_t", e, ic)], inc="vev", tag=("crepmm", e, ic))
                    lastA[ic] = ("crepmm", e, ic)

            def down_half(v_, h):
                e_, tb_ = v_ // NTB, v_ % NTB
                par = e_ % 2
                wsem = "dwa" if par == 0 else "dwb"
                wval = w.ev[("ldw", e_)][1]
                for hcq in range(4):
                    hcol = h * 4 + hcq
                    for ic in range(4):
                        ws = []
                        if hcq == 0 and ic == 0:
                            ws = [("zmul", v_, 3), (wsem, wval)]
                            if h == 0 and v_ >= 1:
                                ws.append(("add", v_ - 1, 1))
                            if h == 1:
                                ws.append(("add", v_, 0))
                        last = hcq == 3 and ic == 3
                        O("T", lambda t, par=par, ic=ic, hcol=hcol, hcq=hcq, v_=v_: t.matmul(
                            psB[:, hcq * 512:(hcq + 1) * 512],
                            wd_sb2[par][:, ic, hcol * 128:(hcol + 1) * 128],
                            z_sb[v_ % 3][:, ic, :], start=(ic == 0), stop=(ic == 3)),
                          waits=ws, inc=("tmm" if last else None),
                          tag=(("mmD", v_, h) if last else None))
                e0 = e_ == 0

                def addfn(v, v_=v_, h=h, tb_=tb_, e0=e0):
                    dst = acc_sb[:, tb_, h, :, :].rearrange("p a b -> p (a b)")
                    if e0:
                        return v.tensor_copy(dst, psB[:, :])
                    return v.tensor_tensor(dst, dst, psB[:, :], ALU.add)
                O("V", addfn, waits=[("mmD", v_, h)], inc="vev", tag=("add", v_, h))

            crep_mms(0)
            emit_crow_load(w, 1, [("crepmm", 0, 3)])

            for u in range(NU):
                e, tb = u // NTB, u % NTB
                par = e % 2
                zb = u % 2
                wsem = "dwa" if par == 0 else "dwb"
                wval = w.ev[("ldw", e)][1]

                if u >= 2:
                    down_half(u - 2, 0)
                if tb == 2 and (e + 1) < 32:
                    crep_mms(e + 1)

                for ic in range(4):
                    for hc in range(HC):
                        ws = []
                        if hc == 0:
                            ws = [lastA[ic], (wsem, wval)]
                            if u == 0 and ic == 0:
                                ws += list(bw)
                        O("T", lambda t, par=par, ic=ic, hc=hc, tb=tb: t.matmul(
                            psA[:, ic * 512:(ic + 1) * 512],
                            wg_sb2[par][:, hc, ic * 128:(ic + 1) * 128],
                            xn_own[:, hc, tb * 512:(tb + 1) * 512],
                            start=(hc == 0), stop=(hc == 7)),
                          waits=ws, inc=("tmm" if hc == 7 else None),
                          tag=(("mmGi", u, ic) if hc == 7 else None))
                    ws = [("mmGi", u, ic)]
                    if u >= 2:
                        ws.append(("zmul", u - 2, ic))
                    O("S", lambda s, zb=zb, ic=ic: s.activation(
                        g_sb[zb][:, ic, :], psA[:, ic * 512:(ic + 1) * 512], AF.Silu),
                      waits=ws, inc="sev", tag=("silu", u, ic))

                if u >= 2:
                    down_half(u - 2, 1)
                if tb == 1 and u >= 5 and (e + 1) < NE:
                    emit_loads(e + 1)

                for ic in range(4):
                    for hc in range(HC):
                        ws = [("silu", u, ic)] if hc == 0 else []
                        O("T", lambda t, par=par, ic=ic, hc=hc, tb=tb: t.matmul(
                            psA[:, ic * 512:(ic + 1) * 512],
                            wu_sb2[par][:, hc, ic * 128:(ic + 1) * 128],
                            xn_own[:, hc, tb * 512:(tb + 1) * 512],
                            start=(hc == 0), stop=(hc == 7)),
                          waits=ws, inc=("tmm" if hc == 7 else None),
                          tag=(("mmUi", u, ic) if hc == 7 else None))
                    if e >= 32:
                        crep_fn = (lambda tb=tb: crep_one[:, tb * 512:(tb + 1) * 512])
                    else:
                        crep_fn = (lambda par=par, tb=tb: crep_sb[par][:, tb, :])
                    ws = [("mmUi", u, ic)]
                    if e >= 32 and ic == 0:
                        ws.append("c_cone")
                    if u >= 2:
                        ws.append(("zmul", u - 2, ic))
                    O("V", lambda v, zb=zb, ic=ic, crep_fn=crep_fn: v.scalar_tensor_tensor(
                        out=u_sb[zb][:, ic, :], in0=psA[:, ic * 512:(ic + 1) * 512],
                        scalar=1.0, in1=crep_fn(), op0=ALU.bypass, op1=ALU.mult),
                      waits=ws, inc="vev", tag=("stt", u, ic))
                    lastA[ic] = ("stt", u, ic)
                for ic in range(4):
                    zw = [("stt", u, ic)]
                    if u >= 3 and ic == 0:
                        zw.append(("mmD", u - 3, 1))
                    O("V", lambda v, u=u, zb=zb, ic=ic: v.tensor_mul(
                        z_sb[u % 3][:, ic, :], g_sb[zb][:, ic, :], u_sb[zb][:, ic, :]),
                      waits=zw, inc="vev", tag=("zmul", u, ic), drain=(ic == 0))

            down_half(NU - 2, 0)
            down_half(NU - 2, 1)
            down_half(NU - 1, 0)
            down_half(NU - 1, 1)

            gtr = [0]
            for tb in range(NTB):
                ws = [("add", 33 * NTB + tb, 0), ("add", 33 * NTB + tb, 1)]
                if tb >= 1:
                    ws.append(("gsilfree", tb - 1))
                O("S", lambda s, tb=tb: s.activation(
                    g_sil[:, :, :, :], acc_sb[:, tb, :, :, :], AF.Silu),
                  waits=ws, inc="sev", tag=("gsil", tb))
                for tt4 in range(4):
                    tpar = (tb * 4 + tt4) % 2
                    for hcol in range(HC):
                        half, hcq = hcol // 4, hcol % 4
                        k = gtr[0] % 4
                        ws = [("gsil", tb), gfree[k]]
                        if gtr[0] == 0:
                            ws.append(("add", NU - 1, 1))
                        if hcol == 0 and (tb * 4 + tt4) >= 2:
                            ws.append(("dgw", 16 * (tb * 4 + tt4 - 1)))
                        O("T", lambda t, tb=tb, half=half, hcq=hcq, tt4=tt4, k=k: t.transpose(
                            psB[:, k * 512:k * 512 + 128],
                            g_sil[:, half, hcq, tt4 * 128:(tt4 + 1) * 128], ident_f32[:, :]),
                          waits=ws, inc="tmm", tag=("gtr", gtr[0]))
                        O("V", lambda v, tpar=tpar, hcol=hcol, k=k: v.tensor_copy(
                            gtm_stage[tpar][:, hcol * 128:(hcol + 1) * 128],
                            psB[:, k * 512:k * 512 + 128]),
                          waits=[("gtr", gtr[0])], inc="vev", tag=("gtrev", gtr[0]))
                        gfree[k] = ("gtrev", gtr[0])
                        gtr[0] += 1
                    O("Y", lambda y, tb=tb, tt4=tt4, tpar=tpar: y.dma_start(
                        gscr_d[:, tb * 4 + tt4, :], gtm_stage[tpar][:, :]),
                      waits=[("gtrev", gtr[0] - 1)], inc="dgw", tag=("stg", tb * 4 + tt4))
                w.ev[("gsilfree", tb)] = w.ev[("gtr", gtr[0] - 1)]

        def ph3(w):
            O = w.op
            bw = w.bar(bar2)
            O("Y", lambda y: y.dma_start(f_fm[:, :, 0:256], ffm_d[:, :, 0:256]),
              waits=bw, inc="dffr", tag="ldffm0")
            O("Y", lambda y: y.dma_start(f_fm[:, :, 256:], ffm_d[:, :, 256:]),
              inc="dffr", tag="ldffm")

            def emit_band_load(st):
                sb_ = st % 2
                bsem = "dba" if sb_ == 0 else "dbb"
                bandw = list(bw) if st < 2 else [("bandlast", st - 2)]
                O("G", lambda g, st=st, sb_=sb_: g.dma_start(
                    bands_sb[sb_][:, :, :], bands_d[st, :, :, :]),
                  waits=bandw, inc=bsem, tag=("ldband", st))

            def emit_g_load(st):
                sb_ = st % 2
                gsem = "dca" if sb_ == 0 else "dcb"
                gw_ = list(bw) if st < 2 else [("fin", st - 2, 1)]
                O("Y", lambda y, st=st, sb_=sb_: y.dma_start(
                    gbuf[sb_][:, :, :], gscr_d[:, st * 2:st * 2 + 2, :]),
                  waits=gw_, inc=gsem, tag=("ldg", st))

            def emit_x_load(st):
                xw_ = list(bw) if st < 1 else [("fin", st - 1, 1)]
                O("Y", lambda y, st=st: y.dma_start(
                    xbuf3[:, :, :],
                    x_d[st * 256:(st + 1) * 256, :].rearrange("(a p) d -> p a d", p=128)),
                  waits=xw_, inc="dxr", tag=("ldx3", st))

            def emit_pv(st, jc):
                pb = jc % 2
                ch, cpar = jc // 4, (jc // 4) % 2
                csem = "dra" if cpar == 0 else "drb"
                pvw = [("exp", st, jc), (csem, w.ev[("ldftm", st, ch)][1])]
                if jc == 0 and st >= 1:
                    pvw.append(("fin", st - 1, 1))
                first, last = jc == 0, jc == NKB - 1
                for q in range(2):
                    for nh in range(2):
                        O("T", lambda t, q=q, nh=nh, pb=pb, cpar=cpar, jc=jc,
                                 first=first, last=last: t.matmul(
                            psAcc[q][:, nh * 512:(nh + 1) * 512],
                            pt_sb[pb][:, q * 128:(q + 1) * 128],
                            ftm_buf[cpar][:, jc % 4, nh * 512:(nh + 1) * 512],
                            start=first, stop=last, skip_group_check=True),
                          waits=pvw if (q == 0 and nh == 0) else [], inc=None)
                    O("T", lambda t, q=q, pb=pb, first=first, last=last: t.matmul(
                        psRS[q][:, 0:1], pt_sb[pb][:, q * 128:(q + 1) * 128],
                        ones_col[:, :], start=first, stop=last, skip_group_check=True),
                      waits=[], inc=("tmm" if q == 1 else None),
                      tag=(("pv", st, jc) if q == 1 else None))

            def emit_finale(st):
                sb_ = st % 2
                O("V", lambda v: v.tensor_copy(rs_sb[:, 0:1], psRS[0][:, 0:1]),
                  waits=[("pv", st, NKB - 1)], inc="vev")
                O("V", lambda v: v.tensor_copy(rs_sb[:, 1:2], psRS[1][:, 0:1]), inc="vev")
                O("V", lambda v: v.reciprocal(rec_sb[:, :], rs_sb[:, :]),
                  inc="vev", drain=True)
                for q in range(2):
                    ws = [("ldg", st)] if q == 0 else []
                    O("V", lambda v, q=q, sb_=sb_: v.scalar_tensor_tensor(
                        out=sg_sb[:, q, :], in0=psAcc[q][:, :], scalar=rec_sb[:, q:q + 1],
                        in1=gbuf[sb_][:, q, :], op0=ALU.mult, op1=ALU.mult),
                      waits=ws, inc="vev", drain=(q == 0))
                for q in range(2):
                    ws = [("dxr", w.ev[("ldx3", st)][1])] if q == 0 else []
                    O("V", lambda v, q=q: v.tensor_add(sg_sb[:, q, :], sg_sb[:, q, :],
                                                       xbuf3[:, q, :]),
                      waits=ws, inc="vev", tag=("fin", st, q))
                O("Y", lambda y, st=st: y.dma_start(
                    out_d[st * 256:(st + 1) * 256, :].rearrange("(a p) d -> p a d", p=128),
                    sg_sb[:, :, :]),
                  waits=[("fin", st, 1)], inc="dor", tag=("stout", st))

            emit_band_load(0)
            emit_band_load(1)
            emit_g_load(0)
            emit_g_load(1)
            emit_x_load(0)
            prev = None
            for st in range(NST):
                sb_ = st % 2
                bsem = "dba" if sb_ == 0 else "dbb"
                bkbs = BAND_KBS[st]
                for jc in range(NKB):
                    pb = jc % 2
                    is_band = jc in bkbs
                    negc = negCn if is_band else negCf
                    for hc in range(HC):
                        ws = []
                        if hc == 0:
                            ws = [("dffr", 16 if (st == 0 and jc < 2) else 32)]
                            if st > 0 or jc >= 2:
                                pst, pjc = (st, jc - 2) if jc >= 2 else (st - 1, NKB - 2 + jc)
                                ws.append(("exp", pst, pjc))
                        O("T", lambda t, pb=pb, hc=hc, jc=jc, st=st: t.matmul(
                            psS[pb][:, :256], f_fm[:, hc, jc * 128:(jc + 1) * 128],
                            f_fm[:, hc, st * 256:(st + 1) * 256],
                            start=(hc == 0), stop=False, skip_group_check=True),
                          waits=ws, inc=None)
                    O("T", lambda t, pb=pb, negc=negc, st=st: t.matmul(
                        psS[pb][:, :256], ones_row[:, :], negc[0:1, st * 256:(st + 1) * 256],
                        start=False, stop=True, skip_group_check=True),
                      waits=[], inc="tmm", tag=("st", st, jc))
                    expw = [("st", st, jc)]
                    if is_band:
                        i = bkbs.index(jc)
                        O("V", lambda v, pb=pb, sb_=sb_, i=i: v.tensor_tensor(
                            psS[pb][:, :256], psS[pb][:, :256], bands_sb[sb_][:, i, :],
                            ALU.add),
                          waits=[("st", st, jc), (bsem, w.ev[("ldband", st)][1])],
                          inc="vev", tag=("bandadd", st, i))
                        expw.append(("bandadd", st, i))
                        w.ev[("bandlast", st)] = w.ev[("bandadd", st, i)]
                    if st > 0 or jc >= 2:
                        pst, pjc = (st, jc - 2) if jc >= 2 else (st - 1, NKB - 2 + jc)
                        expw.append(("pv", pst, pjc))
                    O("S", lambda s, pb=pb: s.activation(pt_sb[pb][:, :], psS[pb][:, :256],
                                                         AF.Exp),
                      waits=expw, inc="sev", tag=("exp", st, jc))
                    if prev is not None:
                        emit_pv(*prev)
                    prev = (st, jc)
                    if jc % 4 == 0:
                        ch = jc // 4
                        cpar = ch % 2
                        csem = "dra" if cpar == 0 else "drb"
                        gch = st * 16 + ch
                        if gch < 2:
                            cw = list(bw)
                        else:
                            pst, pc = (gch - 2) // 16, (gch - 2) % 16
                            cw = [("pv", pst, pc * 4 + 3)]
                        O("Y", lambda y, st=st, ch=ch, cpar=cpar: y.dma_start(
                            ftm_buf[cpar][:, :, :], ftm_d[:, ch * 4:(ch + 1) * 4, :]),
                          waits=cw, inc=csem, tag=("ldftm", st, ch))
                    if jc == 0 and st >= 1:
                        emit_finale(st - 1)
                        emit_x_load(st)
                        if st + 1 < NST:
                            emit_band_load(st + 1)
                            emit_g_load(st + 1)
            emit_pv(NST - 1, NKB - 1)
            emit_finale(NST - 1)

            O("Y", None, waits=[("dor", w.c["dor"])])

        ph12s = top.enter_context(ExitStack())
        xn_own = ph12s.enter_context(nc.sbuf_tensor("xn_own", [128, HC, TOK], BF16))
        combT = ph12s.enter_context(nc.sbuf_tensor("combT", [32, TOK], BF16))
        wg_sb2 = [ph12s.enter_context(nc.sbuf_tensor(f"wg_sb2{i}", [128, HC, NI], BF16))
                  for i in range(2)]
        wu_sb2 = [ph12s.enter_context(nc.sbuf_tensor(f"wu_sb2{i}", [128, HC, NI], BF16))
                  for i in range(2)]
        wd_sb2 = [ph12s.enter_context(nc.sbuf_tensor(f"wd_sb2{i}", [128, 4, D], BF16))
                  for i in range(2)]
        crow = [ph12s.enter_context(nc.sbuf_tensor("crow0", [1, TOK], BF16))]
        crow.append(crow[0])
        with ExitStack() as ph1s:
            P = ph1s.enter_context
            gb_bf = P(nc.sbuf_tensor("gb_bf", [1, 32], BF16))
            wt_sb = P(nc.sbuf_tensor("wt_sb", [128, HC, D], BF16))
            x_stage = [P(nc.sbuf_tensor(f"x_stage{i}", [128, 4, D], F32)) for i in range(2)]
            xn_stage = [P(nc.sbuf_tensor(f"xn_stage{i}", [128, 4, D], BF16)) for i in range(2)]
            xnf_stage = [P(nc.sbuf_tensor(f"xnf_stage{i}", [128, HC, 512], BF16))
                         for i in range(2)]
            f_stage = [P(nc.sbuf_tensor(f"f_stage{i}", [128, HC, 512], BF16)) for i in range(2)]
            ftm_stage = [P(nc.sbuf_tensor(f"ftm_stage{i}", [128, D], BF16)) for i in range(2)]
            fsq = [P(nc.sbuf_tensor(f"fsq{i}", [128, 512], BF16)) for i in range(2)]
            stat_sum = P(nc.sbuf_tensor("stat_sum", [128, 4], F32))
            stat_sumsq = P(nc.sbuf_tensor("stat_sumsq", [128, 4], F32))
            stat_mean = P(nc.sbuf_tensor("stat_mean", [128, 4], F32))
            stat_var = P(nc.sbuf_tensor("stat_var", [128, 4], F32))
            stat_msq = P(nc.sbuf_tensor("stat_msq", [128, 4], F32))
            stat_ln = [P(nc.sbuf_tensor(f"stat_ln{i}", [128, 4], F32)) for i in range(2)]
            stat_nmean = [P(nc.sbuf_tensor(f"stat_nmean{i}", [128, 4], F32)) for i in range(2)]
            stat_rstd = [P(nc.sbuf_tensor(f"stat_rstd{i}", [128, 4], F32)) for i in range(2)]
            stat_nmr = [P(nc.sbuf_tensor(f"stat_nmr{i}", [128, 4], F32)) for i in range(2)]
            logits_sb = P(nc.sbuf_tensor("logits_sb", [128, 16, 32], F32))
            scores_sb = P(nc.sbuf_tensor("scores_sb", [128, 16, 32], F32))
            biasrep = P(nc.sbuf_tensor("biasrep", [128, 32], F32))
            rt_sc = P(nc.sbuf_tensor("rt_sc", [128, 32], F32))
            rt_pair = P(nc.sbuf_tensor("rt_pair", [128, 48], F32))
            rt_m1 = P(nc.sbuf_tensor("rt_m1", [128, 8], F32))
            rt_m2 = P(nc.sbuf_tensor("rt_m2", [128, 8], F32))
            rt_m3 = P(nc.sbuf_tensor("rt_m3", [128, 8], F32))
            rt_gs = P(nc.sbuf_tensor("rt_gs", [128, 8], F32))
            rt_gs8 = P(nc.sbuf_tensor("rt_gs8", [128, 8], F32))
            rt_gmask = P(nc.sbuf_tensor("rt_gmask", [128, 8], F32))
            rt_smask = P(nc.sbuf_tensor("rt_smask", [128, 32], F32))
            rt_masked = P(nc.sbuf_tensor("rt_masked", [128, 32], F32))
            rt_tk8 = P(nc.sbuf_tensor("rt_tk8", [128, 8], F32))
            rt_emask = P(nc.sbuf_tensor("rt_emask", [128, 32], F32))
            rt_w8 = P(nc.sbuf_tensor("rt_w8", [128, 32], F32))
            rt_wsum = P(nc.sbuf_tensor("rt_wsum", [128, 1], F32))
            rt_rec = P(nc.sbuf_tensor("rt_rec", [128, 1], F32))
            rt_comb = P(nc.sbuf_tensor("rt_comb", [128, 16, 32], BF16))
            ps_f = [P(nc.psum_tensor(f"ps_f{i}", [128, 512], F32)) for i in range(2)]
            ps_tr = [P(nc.psum_tensor(f"ps_tr{i}", [128, 1024], BF16)) for i in range(4)]
            ps_nsq = P(nc.psum_tensor("ps_nsq", [128, 512], F32))
            ps_gate = P(nc.psum_tensor("ps_gate", [128, 512], F32))

            def ph1_run(w):
                ph1(w)
                bar1.update(w.snap())
                evc1.update(w.ev)

            with nc.Block() as blk:
                walks(blk, ph1_run)

        if phases >= 2:
            with ExitStack() as ph2s:
                P = ph2s.enter_context
                crep_sb = [P(nc.sbuf_tensor(f"crep_sb{i}", [128, 4, 512], BF16))
                           for i in range(2)]
                crep_one = P(nc.sbuf_tensor("crep_one", [128, TOK], BF16))
                g_sb = [P(nc.sbuf_tensor(f"g_sb{i}", [128, 4, 512], BF16)) for i in range(2)]
                u_sb = [P(nc.sbuf_tensor(f"u_sb{i}", [128, 4, 512], BF16)) for i in range(2)]
                z_sb = [P(nc.sbuf_tensor(f"z_sb{i}", [128, 4, 512], BF16)) for i in range(3)]
                acc_sb = P(nc.sbuf_tensor("acc_sb", [128, NTB, 2, 4, 512], BF16))
                g_sil = P(nc.sbuf_tensor("g_sil", [128, 2, 4, 512], F32))
                gtm_stage = [P(nc.sbuf_tensor(f"gtm_stage{i}", [128, D], BF16))
                             for i in range(2)]
                psA = P(nc.psum_tensor("psA", [128, 2048], F32))
                psB = P(nc.psum_tensor("psB", [128, 2048], F32))

                def ph2_run(w):
                    ph2(w)
                    bar2.update(w.snap())

                with nc.Block() as blk:
                    walks(blk, ph2_run, init=bar1, init_ev=evc1)

        ph12s.close()
        if phases >= 3:
            with ExitStack() as ph3s:
                P = ph3s.enter_context
                f_fm = P(nc.sbuf_tensor("f_fm", [128, HC, S], BF16))
                ftm_buf = [P(nc.sbuf_tensor(f"ftm_buf{i}", [128, 4, D], BF16))
                           for i in range(2)]
                pt_sb = [P(nc.sbuf_tensor(f"pt_sb{i}", [128, 256], BF16)) for i in range(2)]
                bands_sb = [P(nc.sbuf_tensor(f"bands_sb{i}", [128, 4, 256], BF16))
                            for i in range(2)]
                gbuf = [P(nc.sbuf_tensor(f"gbuf{i}", [128, 2, D], BF16)) for i in range(2)]
                xbuf3 = P(nc.sbuf_tensor("xbuf3", [128, 2, D], F32))
                sg_sb = P(nc.sbuf_tensor("sg_sb", [128, 2, D], F32))
                rs_sb = P(nc.sbuf_tensor("rs_sb", [128, 2], F32))
                rec_sb = P(nc.sbuf_tensor("rec_sb", [128, 2], F32))
                rsdbg_sb = P(nc.sbuf_tensor("rsdbg_sb", [128, 4], F32))
                psS = [P(nc.psum_tensor(f"psS{i}", [128, 512], F32)) for i in range(2)]
                psAcc = [P(nc.psum_tensor(f"psAcc{i}", [128, 1024], F32)) for i in range(2)]
                psRS = [P(nc.psum_tensor(f"psRS{i}", [128, 512], F32)) for i in range(2)]

                with nc.Block() as blk:
                    walks(blk, ph3, init=bar2)

    return nc



def _build_bands(off):
    bands = np.zeros((NST, 128, 4, 256), np.float32)
    for st in range(NST):
        for i, kb in enumerate(BAND_KBS[st]):
            kk = np.arange(128)
            qq = np.arange(256)
            j_true = (kb * 128 + kk + off) % S
            q_true = st * 256 + qq + off
            d = np.abs(q_true[None, :] - j_true[:, None]).astype(np.float32)
            bands[st, :, i, :] = BSC * np.minimum(d, float(MAXD))
    return bands


def prep_inputs(L0_attn_x, L0_attn_W, L0_attn_b, L0_ffn_gate_w, L0_ffn_gate_bias,
                L0_ffn_w_gate, L0_ffn_w_up, L0_ffn_w_down, L0_ffn_sh_gate,
                L0_ffn_sh_up, L0_ffn_sh_down):
    x = np.asarray(L0_attn_x, np.float32)
    wt = np.ascontiguousarray(np.asarray(L0_attn_W, np.float32).T)
    battn = np.ascontiguousarray(np.asarray(L0_attn_b, np.float32).reshape(D, 1))
    gw = np.ascontiguousarray(np.asarray(L0_ffn_gate_w, np.float32).T)
    gb = np.ascontiguousarray(np.asarray(L0_ffn_gate_bias, np.float32).reshape(1, 32))

    wg = np.asarray(L0_ffn_w_gate, np.float32)
    wu = np.asarray(L0_ffn_w_up, np.float32)
    wd = np.asarray(L0_ffn_w_down, np.float32)
    shg = np.asarray(L0_ffn_sh_gate, np.float32)
    shu = np.asarray(L0_ffn_sh_up, np.float32)
    shd = np.asarray(L0_ffn_sh_down, np.float32)

    moe = np.empty((NE, 3, NI * D), np.float32)
    for e in range(32):
        moe[e, 0] = wg[e].T.reshape(-1)
        moe[e, 1] = wu[e].T.reshape(-1)
        moe[e, 2] = wd[e].T.reshape(-1)
    shdT = np.ascontiguousarray(shd.T)
    for k in range(2):
        e = 32 + k
        moe[e, 0] = shg[k * NI:(k + 1) * NI, :].T.reshape(-1)
        moe[e, 1] = shu[k * NI:(k + 1) * NI, :].T.reshape(-1)
        moe[e, 2] = shdT[k * NI:(k + 1) * NI, :].reshape(-1)

    bands_cache = {}
    maps = []
    for c in range(8):
        b, off = c // 4, (c % 4) * TOK
        if off not in bands_cache:
            bands_cache[off] = _build_bands(off)
        xr = np.ascontiguousarray(np.roll(x[b], -off, axis=0))
        maps.append(dict(x=xr, wt=wt, battn=battn, gw=gw, gb=gb, moe=moe,
                         bands=bands_cache[off]))
    return maps


def kernel(**inputs):
    in_maps = prep_inputs(**inputs)
    nc = build_program()
    res = run_bass_kernel_spmd(nc, in_maps, core_ids=list(range(8)))
    out = np.empty((B, S, D), np.float32)
    for c in range(8):
        b, off = c // 4, (c % 4) * TOK
        out[b, off:off + TOK, :] = res.results[c]["out"]
    return out
